# version 17
# baseline (speedup 1.0000x reference)
import numpy as np
import ml_dtypes
import concourse.bass as bass
import concourse.mybir as mybir
from concourse.bass_utils import run_bass_kernel_spmd
from concourse.bass import IndirectOffsetOnAxis

F32 = mybir.dt.float32
BF16 = mybir.dt.bfloat16
I32 = mybir.dt.int32
AF = mybir.ActivationFunctionType
ALU = mybir.AluOpType
AXX = mybir.AxisListType.X

T = 2048
NCH = 2048
RQ = 1024
NM = 64
NDC = 16
HDIM = 8192
EPS = 1e-5
FWHT_SCALE = float(1.0 / np.sqrt(8192.0))
INV_T = float(1.0 / 2048.0)

OFF0 = 28608
A_OFF = OFF0
B_OFF = OFF0 + 131072
C_OFF = OFF0 + 163840
SM_OFF = C_OFF + 32768


def _host_prepare(bv, cols, rows, W1, W2, W3):
    cols = np.asarray(cols); rows = np.asarray(rows)
    order = np.argsort(cols, kind="stable")
    groups = []
    i = 0
    while i < len(order):
        b = int(order[i])
        if i + 1 < len(order) and cols[order[i + 1]] == cols[b]:
            groups.append((int(cols[b]), b, int(order[i + 1])))
            i += 2
        else:
            groups.append((int(cols[b]), b, None))
            i += 1
    G = len(groups)
    bvp = np.zeros((G, 128, 8, 128), np.float32)
    for g, (c, bA, bB) in enumerate(groups):
        for ph, b in enumerate((bA, bB)):
            if b is None:
                continue
            w = bv[b].transpose(2, 1, 0).reshape(8, 2, 64, 64)
            w = w.transpose(1, 2, 0, 3).reshape(128, 8, 64)
            bvp[g, :, :, ph * 64:(ph + 1) * 64] = w

    p_ = np.arange(128)
    m_ = np.arange(64)
    h_of = (p_[None, :] >> 1) * 128 + 2 * m_[:, None] + (p_[None, :] & 1)
    w1t = W1.reshape(HDIM, 16, 128)[h_of].transpose(0, 3, 2, 1)
    w3t = W3.reshape(HDIM, 16, 128)[h_of].transpose(0, 3, 2, 1)
    wcat = np.stack([w1t.reshape(64, 128, 2048), w3t.reshape(64, 128, 2048)],
                    axis=2)
    wcat = np.ascontiguousarray(wcat.reshape(64, 128, 4096))

    w2v = W2.T[h_of]
    w2v = w2v.reshape(16, 4, 128, 4, 512)
    w2t = np.ascontiguousarray(
        w2v.transpose(3, 0, 2, 1, 4).reshape(4, 16, 128, 2048))

    return groups, bvp, wcat, w2t


def _hada(n):
    h = np.array([[1.0]], np.float32)
    while h.shape[0] < n:
        h = np.block([[h, h], [h, -h]])
    return h


def _build(groups, rows, dbg=False, dbg2=False):
    G = len(groups)
    srcs = sorted({c for (c, _, _) in groups})
    NS = len(srcs)
    sidx_of = {c: i for i, c in enumerate(srcs)}
    gsidx = [sidx_of[c] for (c, _, _) in groups]
    last_g_of_sidx = {}
    for g, si in enumerate(gsidx):
        last_g_of_sidx[si] = g
    targets = []
    seen = set()
    for g, (c, bA, bB) in enumerate(groups):
        tg = []
        for ph, b in enumerate((bA, bB)):
            if b is None:
                continue
            r = int(rows[b])
            tg.append((ph, r, r not in seen))
            seen.add(r)
        targets.append(tg)
    written_halves = set(seen)

    last_g_of_tile = {}
    for g, tg in enumerate(targets):
        for (ph, r, first) in tg:
            last_g_of_tile[r // 2] = g
    touched = sorted(last_g_of_tile)
    untouchedT = [Ti for Ti in range(16) if Ti not in last_g_of_tile]
    copy_order = untouchedT + sorted(touched, key=lambda Ti: last_g_of_tile[Ti])
    copy_pos = {Ti: p for p, Ti in enumerate(copy_order)}
    acc_order = sorted(touched, key=lambda Ti: last_g_of_tile[Ti])
    acc_pos = {Ti: p for p, Ti in enumerate(acc_order)}

    nc = bass.Bass("TRN2", target_bir_lowering=False,
                   detect_race_conditions=False)

    xb = nc.dram_tensor("xb", [NCH, T], F32, kind="ExternalInput").ap()
    bvp = nc.dram_tensor("bvp", [G, 128, 8, 128], F32, kind="ExternalInput").ap()
    wcat = nc.dram_tensor("wcat", [64, 128, 4096], F32, kind="ExternalInput").ap()
    w2t = nc.dram_tensor("w2t", [4, 16, 128, 2048], F32, kind="ExternalInput").ap()
    hcin = nc.dram_tensor("hcin", [128, 128], BF16, kind="ExternalInput").ap()
    idin = nc.dram_tensor("idin", [128, 128], BF16, kind="ExternalInput").ap()
    xgidx = nc.dram_tensor("xgidx", [128, 8], I32, kind="ExternalInput").ap()
    x1full = nc.dram_tensor("x1full", [NCH, T], F32, kind="Internal").ap()
    out = nc.dram_tensor("out", [RQ, T], F32, kind="ExternalOutput").ap()
    if dbg2:
        d_x1 = nc.dram_tensor("d_x1", [NCH, T], F32, kind="ExternalOutput").ap()
        d_hid = nc.dram_tensor("d_hid", [128, 65536], BF16,
                               kind="ExternalOutput").ap()
        d_sm = nc.dram_tensor("d_sm", [128, 16], F32, kind="ExternalOutput").ap()
        d_xnt = nc.dram_tensor("d_xnt", [128, 16384], BF16,
                               kind="ExternalOutput").ap()
        d_sg = nc.dram_tensor("d_sg", [128, 1024], F32, kind="ExternalOutput").ap()
        d_wb = nc.dram_tensor("d_wb", [2, 128, 4096], BF16,
                              kind="ExternalOutput").ap()
        d_x1b = nc.dram_tensor("d_x1b", [NCH, T], F32, kind="ExternalOutput").ap()
        d_log = nc.dram_tensor("d_log", [128, 256], F32, kind="ExternalOutput").ap()
        d_ya = nc.dram_tensor("d_ya", [128, 32768], F32, kind="ExternalOutput").ap()
    if dbg:
        d_x1 = nc.dram_tensor("d_x1", [NCH, T], F32, kind="ExternalOutput").ap()
        d_xnt = nc.dram_tensor("d_xnt", [128, 16384], BF16,
                               kind="ExternalOutput").ap()
        d_hid = nc.dram_tensor("d_hid", [128, 65536], BF16,
                               kind="ExternalOutput").ap()
        d_blk = nc.dram_tensor("d_blk", [4, 128, 2064], BF16,
                               kind="ExternalOutput").ap()
        d_sm = nc.dram_tensor("d_sm", [128, 16], F32, kind="ExternalOutput").ap()

    at = nc.alloc_sbuf_tensor_at
    YT = at("YT", [128, 16 * 2048], F32, offset=A_OFF)
    HID = at("HID", [128, 64 * 1024], BF16, offset=A_OFF)
    xblk = [at(f"xblk{i}", [128, 2064], BF16, offset=B_OFF + i * 4160)
            for i in range(4)]
    XnT = at("XnT", [128, 16384], BF16, offset=B_OFF)
    _xdoff = [0, 8192, 24832]
    x1ds = [at(f"x1ds{i}", [128, 2048], F32, offset=B_OFF + _xdoff[i])
            for i in range(3)]
    xsrc = [at(f"xsrc{i}", [128, 2048], F32, offset=C_OFF + i * 8192) for i in range(2)]
    xbt = [at(f"xbt{i}", [128, 2048], F32, offset=C_OFF + i * 8192) for i in range(2)]
    x1rt = [at(f"x1rt{i}", [128, 2048], F32, offset=C_OFF + i * 8192) for i in range(2)]
    x1rt.append(at("x1rt2", [128, 2048], F32, offset=A_OFF + 15 * 8192))
    wbuf = [at(f"wbuf{i}", [128, 4096], BF16, offset=C_OFF + i * 8192) for i in range(2)]
    lntmp = [at(f"lntmp{i}", [128, 2048], BF16, offset=C_OFF + 16384 + i * 4096)
             for i in range(2)]
    xn = [at(f"xn{i}", [128, 2048], BF16, offset=C_OFF + 16384 + i * 4096)
          for i in range(2)]
    sg = at("sg", [128, 1024], F32, offset=C_OFF + 16384)
    _w2off = [16384, 20480, 24576, 28672]
    w2b = [at(f"w2b{i}", [128, 2048], BF16, offset=C_OFF + _w2off[i])
           for i in range(4)]
    junk = at("junk", [128, 2048], F32, offset=C_OFF + 24576)
    obuf = [at(f"obuf{i}", [128, 512], F32, offset=B_OFF + 20736 + i * 2048)
            for i in range(2)]
    bvw = [at(f"bvw{i}", [128, 1024], BF16, offset=B_OFF + 16640 + i * 2048)
           for i in range(2)]
    ssum = at("ssum", [128, 2], F32, offset=SM_OFF)
    ssq = at("ssq", [128, 2], F32, offset=SM_OFF + 32)
    smean = at("smean", [128, 2], F32, offset=SM_OFF + 64)
    sm2 = at("sm2", [128, 2], F32, offset=SM_OFF + 96)
    svar = at("svar", [128, 2], F32, offset=SM_OFF + 128)
    sstd = at("sstd", [128, 2], F32, offset=SM_OFF + 160)
    srstd = at("srstd", [128, 2], F32, offset=SM_OFF + 192)
    snmr = at("snmr", [128, 2], F32, offset=SM_OFF + 224)
    hc_sb = at("hc_sb", [128, 128], BF16, offset=SM_OFF + 256)
    id_sb = at("id_sb", [128, 128], BF16, offset=SM_OFF + 512)
    idx_sb = at("idx_sb", [128, 8], I32, offset=SM_OFF + 768)
    seps = at("seps", [128, 1], F32, offset=SM_OFF + 960)
    slog = at("slog", [128, 256], F32, offset=SM_OFF + 1024)

    pA = nc.alloc_psum_tensor("pA", [128, 2048], F32)
    pB = nc.alloc_psum_tensor("pB", [128, 2048], F32)
    pgrp = [pA, pB]
    ptx = [pA, pB]
    pg = [pA[:, 0:1024], pB[:, 0:1024]]
    pu = [pA[:, 1024:2048], pB[:, 1024:2048]]

    def pds(rt):
        base = pA if rt < 4 else pB
        return base[:, (rt % 4) * 512:(rt % 4) * 512 + 512]

    semnames = ["init", "xsrc0", "xsrc1", "bvw0", "bvw1", "acc", "var",
                "sqrt", "nmr", "lntmp", "xblk", "grpmm", "evac", "msk",
                "xcp", "gch", "sch", "x1f", "gat0", "gat1",
                "xn", "tx", "xnt", "wd0", "wd1", "gu", "silu", "mul", "hc",
                "hce", "bf", "w2q0", "w2q1", "w2q2", "w2q3", "gds0", "gds1", "gds2", "gat2", "dmm", "oev",
                "st0", "st1", "dbgx", "vch", "dmmf", "gg"]
    SEM = {n: nc.alloc_semaphore(f"s_{n}") for n in semnames}
    trace = {k: [] for k in ["sync", "gpsimd", "tensor", "scalar", "vector"]}

    def tile_ready(eng, key, Ti):
        W(eng, key, "xcp", 16 * (copy_pos[Ti] + 1))
        if Ti in last_g_of_tile:
            W(eng, key, "x1f", 16 * (acc_pos[Ti] + 1))

    def W(eng, key, sem, val):
        if val <= 0:
            return
        eng.wait_ge(SEM[sem], val)
        trace[key].append(("w", sem, val))

    def I(instr, key, sem, n=1):
        instr.then_inc(SEM[sem], n)
        trace[key].append(("i", sem, n))
        return instr

    blk_ctx = nc.Block()
    blk = blk_ctx.__enter__()

    def emit_sync(sy):
        k = "sync"
        I(sy.dma_start(hc_sb[:], hcin), k, "init", 16)
        I(sy.dma_start(id_sb[:], idin), k, "init", 16)
        I(sy.dma_start(idx_sb[:], xgidx), k, "init", 16)
        for si in range(NS):
            c = srcs[si]
            W(sy, k, "lntmp", si - 1)
            I(sy.dma_start(xsrc[si % 2][0:64, :], xb[c * 64:(c + 1) * 64, :]),
              k, f"xsrc{si % 2}", 16)
            I(sy.dma_start(xsrc[si % 2][64:128, :], xb[c * 64:(c + 1) * 64, :]),
              k, f"xsrc{si % 2}", 16)
        if dbg2:
            W(sy, k, "evac", G)
            I(sy.dma_start(d_ya, YT[:]), k, "dbgx", 16)
        if dbg:
            W(sy, k, "xblk", NS)
            for i in range(4):
                I(sy.dma_start(d_blk[i], xblk[i][:]), k, "dbgx", 16)
            W(sy, k, "xnt", 8)
            I(sy.dma_start(d_xnt, XnT[:]), k, "dbgx", 16)
        if dbg2:
            W(sy, k, "x1f", 16 * len(acc_order))
            I(sy.dma_start(d_x1b, x1full), k, "dbgx", 16)
            W(sy, k, "xnt", 8)
            I(sy.dma_start(d_xnt, XnT[:]), k, "dbgx", 16)
        for ps in range(2):
            for ds in range(4):
                for rtl in range(4):
                    ko = (ps * 4 + ds) * 4 + rtl
                    rt = ps * 4 + rtl
                    W(sy, k, "oev", ko + 1)
                    I(sy.dma_start(out[rt * 128:(rt + 1) * 128,
                                       ds * 512:(ds + 1) * 512],
                                   obuf[ko % 2][:]),
                      k, f"st{ko % 2}", 16)
        W(sy, k, "st0", 16 * 16)
        W(sy, k, "st1", 16 * 16)
        if dbg2:
            I(sy.dma_start(d_x1, x1full), k, "dbgx", 16)
            I(sy.dma_start(d_hid, HID[:]), k, "dbgx", 16)
            I(sy.dma_start(d_sg, sg[:]), k, "dbgx", 16)
            I(sy.dma_start(d_log, slog[:]), k, "dbgx", 16)
            I(sy.dma_start(d_wb[0], wbuf[0][:]), k, "dbgx", 16)
            I(sy.dma_start(d_wb[1], wbuf[1][:]), k, "dbgx", 16)
            for j, tens in enumerate([ssum, ssq, smean, sm2, svar, sstd,
                                      srstd, snmr]):
                I(sy.dma_start(d_sm[:, j * 2:j * 2 + 2], tens[:]), k, "dbgx", 16)
            W(sy, k, "dbgx", 16 * 10)
        if dbg:
            I(sy.dma_start(d_x1, x1full), k, "dbgx", 16)
            I(sy.dma_start(d_hid, HID[:]), k, "dbgx", 16)
            for j, tens in enumerate([ssum, ssq, smean, sm2, svar, sstd,
                                      srstd, snmr]):
                I(sy.dma_start(d_sm[:, j * 2:j * 2 + 2], tens[:]), k, "dbgx", 16)
            W(sy, k, "dbgx", 16 * 15)

    def emit_gpsimd(g):
        k = "gpsimd"
        gcnt = [0]

        def ggate(instr):
            instr.then_inc(SEM["gch"], 1)
            trace[k].append(("i", "gch", 1))
            gcnt[0] += 1
            g.wait_ge(SEM["gch"], gcnt[0])
            trace[k].append(("w", "gch", gcnt[0]))

        for i in range(4):
            I(g.memset(xblk[i][:], 0.0), k, "msk", 1)
        ggate(g.memset(seps[:], EPS))

        done_var, done_nmr, done_cp = set(), set(), set()

        def gvar(i):
            if i in done_var or not (0 <= i < NTOT):
                return
            done_var.add(i)
            sp = i % 2
            W(g, k, "sqrt", i - 1)
            W(g, k, "acc", i + 1)
            ggate(g.tensor_mul(sm2[:, sp:sp + 1], ssum[:, sp:sp + 1],
                               ssum[:, sp:sp + 1]))
            ggate(g.tensor_sub(svar[:, sp:sp + 1], ssq[:, sp:sp + 1],
                               sm2[:, sp:sp + 1]))
            I(g.tensor_add(svar[:, sp:sp + 1], svar[:, sp:sp + 1],
                           seps[:, 0:1]), k, "var", 1)

        def gnmr(i):
            if i in done_nmr or not (0 <= i < NTOT):
                return
            done_nmr.add(i)
            sp = i % 2
            if i >= 2:
                if i - 2 < NS:
                    W(g, k, "lntmp", i - 1)
                else:
                    W(g, k, "xn", i - NS - 1)
            W(g, k, "sqrt", i + 1)
            I(g.tensor_mul(snmr[:, sp:sp + 1], ssum[:, sp:sp + 1],
                           srstd[:, sp:sp + 1]), k, "nmr", 1)

        def gcp(i):
            if i in done_cp or not (0 <= i < NS):
                return
            done_cp.add(i)
            sl = xblk[i % 4]
            W(g, k, "lntmp", i + 1)
            if i >= 4:
                W(g, k, "grpmm", last_g_of_sidx[i - 4] + 1)
            g.tensor_copy(sl[0:64, 15:2063], lntmp[i % 2][0:64, :])
            I(g.tensor_copy(sl[64:128, 14:2062], lntmp[i % 2][64:128, :]),
              k, "xblk", 1)

        def gstep(i):
            gvar(i)
            gnmr(i - 1)
            gcp(i - 2)

        acc_by_g = {}
        for Ti in acc_order:
            acc_by_g.setdefault(min(last_g_of_tile[Ti] + 5, G - 1), []).append(Ti)

        si_done = [0]
        for gi in range(G):
            tgt = min(NS, gsidx[gi] + 3)
            while si_done[0] < tgt:
                gstep(si_done[0])
                si_done[0] += 1
            for j in range(gsidx[gi] + 1):
                if j not in done_cp:
                    gvar(j)
                    gnmr(j)
                    gcp(j)
            W(g, k, "grpmm", gi - 1)
            I(g.dma_start(bvw[gi % 2][:], bvp[gi].rearrange("p t f -> p (t f)")),
              k, f"bvw{gi % 2}", 16)
            for Ti in acc_by_g.get(gi, []):
                W(g, k, "evac", gi + 1)
                W(g, k, "xcp", 16 * (copy_pos[Ti] + 1))
                I(g.dma_start(x1full[Ti * 128:(Ti + 1) * 128, :],
                              YT[:, Ti * 2048:(Ti + 1) * 2048],
                              accum_op=ALU.add),
                  k, "x1f", 16)
        while si_done[0] < NS:
            gstep(si_done[0])
            si_done[0] += 1
        gnmr(NS - 1)
        gcp(NS - 2)
        gcp(NS - 1)
        for rt in range(8):
            if rt == 0:
                W(g, k, "init", 48)
            W(g, k, "xn", rt - 2)
            tile_ready(g, k, rt)
            tile_ready(g, k, 8 + rt)
            if rt % 3 == 2:
                W(g, k, "x1f", 16 * len(acc_order))
            I(g.indirect_dma_start(x1rt[rt % 3][:], None, x1full,
                                   IndirectOffsetOnAxis(idx_sb[:, rt:rt + 1], 0)),
              k, f"gat{rt % 3}", 16)
            gstep(NS + rt)
        gnmr(NTOT - 1)
        for m in range(NM):
            if m < 2:
                W(g, k, "xn", 8)
            W(g, k, "gu", m - 1)
            I(g.dma_start(wbuf[m % 2][:], wcat[m]), k, f"wd{m % 2}", 16)
        for ps in range(2):
            for ds in range(4):
                ig = ps * 4 + ds
                if ig < 3:
                    W(g, k, "gu", 64)
                    if dbg:
                        W(g, k, "dbgx", 16 * 5)
                    if dbg2:
                        W(g, k, "dbgx", 16 * 2)
                W(g, k, "oev", 4 * (ig - 2))
                for rtl in range(4):
                    rt = ps * 4 + rtl
                    I(g.indirect_dma_start(
                        x1ds[ig % 3][:, rtl * 512:(rtl + 1) * 512],
                        None, x1full,
                        IndirectOffsetOnAxis(idx_sb[:, rt:rt + 1], 0),
                        element_offset=ds * 512),
                      k, f"gds{ig % 3}", 16)
                for gq in range(16):
                    idx = ig * 16 + gq
                    if idx < 4:
                        W(g, k, "mul", 64)
                    W(g, k, "dmm", idx - 3)
                    I(g.dma_start(w2b[idx % 4][:], w2t[ds, gq]),
                      k, f"w2q{idx % 4}", 16)

    def emit_tensor(t):
        k = "tensor"

        def mm(out_ap, lhsT_ap, rhs_ap, start, stop, skip_ldw=False):
            tmpl = t.matmul(out_ap, lhsT_ap, rhs_ap, start=start, stop=stop)
            if not skip_ldw:
                return tmpl
            ins = tmpl.ins
            newi = mybir.InstMatmult(
                name=ins.name, ins=list(ins.ins), outs=list(ins.outs),
                ldweights=False,
                replication_resolution=ins.replication_resolution,
                replication_shift_amnt=ins.replication_shift_amnt,
                replication_num_rows=ins.replication_num_rows,
                start_tensor_calc=ins.start_tensor_calc,
                stop_tensor_calc=ins.stop_tensor_calc,
                engine=ins.engine)
            insts = nc.cur_bb.bb.instructions
            for j in range(len(insts) - 1, -1, -1):
                if insts[j].name == ins.name:
                    insts[j] = newi
                    break
            nc.register_instruction(newi, overwrite=True)
            return bass.BassInstruction(newi)
        for gi in range(G):
            if gi == 0:
                W(t, k, "msk", 4)
            W(t, k, f"bvw{gi % 2}", 16 * (gi // 2 + 1))
            W(t, k, "xblk", gsidx[gi] + 1)
            W(t, k, "evac", gi - 1)
            sl = xblk[gsidx[gi] % 4]
            for tp in range(8):
                for ts in range(4):
                    ins = mm(pgrp[gi % 2][:, ts * 512:(ts + 1) * 512],
                             bvw[gi % 2][:, tp * 128:(tp + 1) * 128],
                             sl[:, 2 * tp + ts * 512:2 * tp + ts * 512 + 512],
                             start=(tp == 0), stop=(tp == 7), skip_ldw=(ts > 0))
            I(ins, k, "grpmm", 1)
        for rt in range(8):
            if rt < 2:
                W(t, k, "evac", G)
                W(t, k, "init", 48)
            W(t, k, "xn", rt + 1)
            W(t, k, "xnt", rt - 1)
            for dc in range(16):
                ins = t.matmul(ptx[rt % 2][:, dc * 128:(dc + 1) * 128],
                               xn[rt % 2][:, dc * 128:(dc + 1) * 128],
                               id_sb[:], start=True, stop=True)
            I(ins, k, "tx", 1)
        def emit_hc(m):
            W(t, k, "mul", m + 1)
            par = m % 2
            for rh in range(2):
                ins = mm(pu[par][:, rh * 512:(rh + 1) * 512],
                         hc_sb[:],
                         HID[:, m * 1024 + rh * 512:m * 1024 + rh * 512 + 512],
                         start=True, stop=True, skip_ldw=(rh == 1))
            I(ins, k, "hc", 1)

        for m in range(NM):
            if m == 0:
                W(t, k, "xnt", 8)
            W(t, k, f"wd{m % 2}", 16 * (m // 2 + 1))
            W(t, k, "hce", m - 1)
            W(t, k, "silu", m - 1)
            par = m % 2
            for dc in range(16):
                for rh in range(2):
                    ins = mm(pg[par][:, rh * 512:(rh + 1) * 512],
                             wbuf[m % 2][:, dc * 128:(dc + 1) * 128],
                             XnT[:, dc * 1024 + rh * 512:dc * 1024 + rh * 512 + 512],
                             start=(dc == 0), stop=(dc == 15), skip_ldw=(rh == 1))
            I(ins, k, "gg", 1)
            if m >= 1:
                emit_hc(m - 1)
            for dc in range(16):
                for rh in range(2):
                    ins = mm(pu[par][:, rh * 512:(rh + 1) * 512],
                             wbuf[m % 2][:, 2048 + dc * 128:2048 + (dc + 1) * 128],
                             XnT[:, dc * 1024 + rh * 512:dc * 1024 + rh * 512 + 512],
                             start=(dc == 0), stop=(dc == 15), skip_ldw=(rh == 1))
            I(ins, k, "gu", 1)
        emit_hc(NM - 1)
        for ps in range(2):
            W(t, k, "bf", ps + 1)
            for ds in range(4):
                ig = ps * 4 + ds
                for gq in range(16):
                    idx = ig * 16 + gq
                    W(t, k, f"w2q{idx % 4}", 16 * (idx // 4 + 1))
                    if gq == 0 and ds >= 1:
                        for rtl in range(4):
                            W(t, k, "oev", (ig - 1) * 4 + rtl + 1)
                    for gi4 in range(4):
                        gg = gq * 4 + gi4
                        for rtl in range(4):
                            rt = ps * 4 + rtl
                            ins = t.matmul(
                                pds(rt),
                                HID[:, gg * 1024 + rt * 128:gg * 1024 + (rt + 1) * 128],
                                w2b[idx % 4][:, gi4 * 512:(gi4 + 1) * 512],
                                start=(gq == 0 and gi4 == 0),
                                stop=(gq == 15 and gi4 == 3))
                            if gq == 15 and gi4 == 3 and rtl < 3:
                                I(ins, k, "dmmf", 1)
                    I(ins, k, "dmm", 1)

    NTOT = NS + 8

    def stat_src(i):
        if i < NS:
            return xsrc[i % 2], i % 2
        return x1rt[(i - NS) % 3], (i - NS) % 2

    def emit_scalar(s):
        k = "scalar"
        scnt = [0]

        def sgate(instr):
            instr.then_inc(SEM["sch"], 1)
            trace[k].append(("i", "sch", 1))
            scnt[0] += 1
            s.wait_ge(SEM["sch"], scnt[0])
            trace[k].append(("w", "sch", scnt[0]))

        for Ti in copy_order:
            I(s.dma_start(x1full[Ti * 128:(Ti + 1) * 128, :],
                          xb[Ti * 128:(Ti + 1) * 128, :]), k, "xcp", 16)

        def acc(i):
            buf, _ = stat_src(i)
            sp = i % 2
            W(s, k, "var", i - 1)
            if i < NS:
                W(s, k, f"xsrc{i % 2}", 32 * (i // 2 + 1))
            else:
                rt = i - NS
                W(s, k, f"gat{rt % 3}", 16 * (rt // 3 + 1))
            s.activation(junk[:], buf[:], AF.Square,
                         scale=0.02209708691207961,
                         accum_out=ssq[:, sp:sp + 1])
            I(s.activation(junk[:], buf[:], AF.Identity, scale=-INV_T,
                           accum_out=ssum[:, sp:sp + 1]), k, "acc", 1)

        def sqrt_(i):
            sp = i % 2
            W(s, k, "nmr", i - 1)
            W(s, k, "var", i + 1)
            sgate(s.activation(sstd[:, sp:sp + 1], svar[:, sp:sp + 1], AF.Ln))
            I(s.activation(srstd[:, sp:sp + 1], sstd[:, sp:sp + 1], AF.Exp,
                           scale=-0.5), k, "sqrt", 1)

        def norm(i):
            buf, bp = stat_src(i)
            sp = i % 2
            W(s, k, "nmr", i + 1)
            if i < NS:
                W(s, k, "xblk", i - 1)
                I(s.activation(lntmp[bp][:], buf[:], AF.Identity,
                               bias=snmr[:, sp:sp + 1],
                               scale=srstd[:, sp:sp + 1]), k, "lntmp", 1)
            else:
                rt = i - NS
                if rt < 2:
                    W(s, k, "xblk", NS)
                W(s, k, "tx", rt - 1)
                I(s.activation(xn[bp][:], buf[:], AF.Identity,
                               bias=snmr[:, sp:sp + 1],
                               scale=srstd[:, sp:sp + 1]), k, "xn", 1)

        done_sq, done_nm = set(), set()

        def sqrt_once(i):
            if 0 <= i < NTOT and i not in done_sq:
                done_sq.add(i)
                sqrt_(i)

        def norm_once(i):
            if 0 <= i < NTOT and i not in done_nm:
                done_nm.add(i)
                norm(i)

        for i in range(NTOT):
            if i == NS:
                sqrt_once(i - 1)
                norm_once(i - 1)
            acc(i)
            if i >= 1:
                sqrt_once(i - 1)
                norm_once(i - 1)
        sqrt_once(NTOT - 1)
        norm_once(NTOT - 1)
        for m in range(NM):
            W(s, k, "gg", m + 1)
            W(s, k, "mul", m)
            I(s.activation(sg[:], pg[m % 2][:, :], AF.Silu), k, "silu", 1)

    def emit_vector(v):
        k = "vector"
        vcnt = [0]

        def vgate(instr):
            instr.then_inc(SEM["vch"], 1)
            trace[k].append(("i", "vch", 1))
            vcnt[0] += 1
            v.wait_ge(SEM["vch"], vcnt[0])
            trace[k].append(("w", "vch", vcnt[0]))

        def evac(gi):
            W(v, k, "grpmm", gi + 1)
            for (ph, r, first) in targets[gi]:
                dst = YT[(r % 2) * 64:(r % 2) * 64 + 64,
                         (r // 2) * 2048:(r // 2 + 1) * 2048]
                src = pgrp[gi % 2][ph * 64:ph * 64 + 64, :]
                if first:
                    vgate(v.tensor_copy(dst, src))
                else:
                    vgate(v.tensor_tensor(dst, src, dst, ALU.add))
            I(v.engine_nop(), k, "evac", 1)

        def txe(rt):
            W(v, k, "tx", rt + 1)
            dst = XnT.rearrange("p (dc r) -> p dc r", dc=16, r=1024)[
                :, :, rt * 128:(rt + 1) * 128]
            I(v.tensor_copy(dst, ptx[rt % 2][:, :].rearrange(
                "p (dc rr) -> p dc rr", dc=16, rr=128)), k, "xnt", 1)

        for gi in range(G):
            evac(gi)
        for rt in range(8):
            txe(rt)
        import os as _os
        _nstage = 0 if _os.environ.get("K_NOBFLY") else 6

        def bfly_op(b, hoi, half):
            d = 1 << b
            r0 = half * 512
            sub = HID[:, hoi * 2 * d * 1024:(hoi + 1) * 2 * d * 1024]
            view = sub.rearrange("p (hp hl r) -> p hp hl r", hp=2, hl=d, r=1024)
            Aap = view[:, 0, :, r0:r0 + 512]
            Bap = view[:, 1, :, r0:r0 + 512]
            vgate(v.tensor_tensor(Bap, Aap, Bap, ALU.subtract))
            vgate(v.scalar_tensor_tensor(Aap, Aap, 2.0, Bap,
                                         ALU.mult, ALU.subtract))

        def bfly_ready(m):
            for b in range(_nstage):
                d = 1 << b
                if (m + 1) % (2 * d) == 0:
                    yield b, (m + 1) // (2 * d) - 1

        def hce(m):
            W(v, k, "hc", m + 1)
            I(v.tensor_scalar_mul(HID[:, m * 1024:(m + 1) * 1024],
                                  pu[m % 2][:, :], FWHT_SCALE), k, "hce", 1)

        for m in range(NM):
            if m >= 1:
                hce(m - 1)
                if m - 1 < NM - 1:
                    for b, hoi in bfly_ready(m - 1):
                        bfly_op(b, hoi, 0)
                        bfly_op(b, hoi, 1)
            W(v, k, "silu", m + 1)
            W(v, k, "gu", m + 1)
            I(v.tensor_mul(HID[:, m * 1024:(m + 1) * 1024], sg[:],
                           pu[m % 2][:, :]), k, "mul", 1)
        hce(NM - 1)
        for half in range(2):
            for b, hoi in bfly_ready(NM - 1):
                bfly_op(b, hoi, half)
            I(v.engine_nop(), k, "bf", 1)
        for ps in range(2):
            for ds in range(4):
                ig = ps * 4 + ds
                W(v, k, f"gds{ig % 3}", 64 * (ig // 3 + 1))
                for rtl in range(4):
                    ko = ig * 4 + rtl
                    if ko >= 2:
                        W(v, k, f"st{ko % 2}", 16 * (ko // 2))
                    if rtl < 3:
                        W(v, k, "dmmf", ig * 3 + rtl + 1)
                    else:
                        W(v, k, "dmm", ig * 16 + 16)
                    I(v.tensor_tensor(obuf[ko % 2][:], pds(ps * 4 + rtl),
                                      x1ds[ig % 3][:, rtl * 512:(rtl + 1) * 512],
                                      ALU.add), k, "oev", 1)

    blk.sync(emit_sync)
    blk.gpsimd(emit_gpsimd)
    blk.tensor(emit_tensor)
    blk.scalar(emit_scalar)
    blk.vector(emit_vector)
    blk_ctx.__exit__(None, None, None)

    _simulate(trace)
    return nc


def _simulate(trace):
    from collections import deque, defaultdict
    qs = {kk: deque(vv) for kk, vv in trace.items()}
    val = defaultdict(int)
    progress = True
    while progress:
        progress = False
        for kk, q in qs.items():
            while q:
                tok = q[0]
                if tok[0] == "w" and val[tok[1]] < tok[2]:
                    break
                if tok[0] == "i":
                    val[tok[1]] += tok[2]
                q.popleft()
                progress = True
    stuck = {kk: (len(q), q[0]) for kk, q in qs.items() if q}
    if stuck:
        raise RuntimeError(f"semaphore deadlock: {stuck}; sems={dict(val)}")


def _enable_ldw_opt():
    return


def kernel(L0_attn_x, L0_attn_block_values, L0_attn_cols, L0_attn_rows,
           L0_ffn_W1, L0_ffn_W2, L0_ffn_W3):
    _enable_ldw_opt()
    x = np.ascontiguousarray(np.asarray(L0_attn_x, dtype=np.float32))
    bv = np.ascontiguousarray(np.asarray(L0_attn_block_values, dtype=np.float32))
    cols = np.asarray(L0_attn_cols)
    rows = np.asarray(L0_attn_rows)
    W1 = np.ascontiguousarray(np.asarray(L0_ffn_W1, dtype=np.float32))
    W2 = np.ascontiguousarray(np.asarray(L0_ffn_W2, dtype=np.float32))
    W3 = np.ascontiguousarray(np.asarray(L0_ffn_W3, dtype=np.float32))

    groups, bvp, wcat, w2t = _host_prepare(bv, cols, rows, W1, W2, W3)
    nc = _build(groups, rows)

    hc_np = np.ascontiguousarray(
        np.kron(_hada(64), _hada(2)).astype(ml_dtypes.bfloat16))
    id_np = np.eye(128, dtype=ml_dtypes.bfloat16)

    in_maps = []
    for core in range(8):
        bi, hf = core // 2, core % 2
        xg = (1024 * hf + np.arange(8)[None, :] * 128
              + np.arange(128)[:, None]).astype(np.int32)
        in_maps.append({
            "xb": x[bi],
            "bvp": bvp,
            "wcat": wcat,
            "w2t": w2t,
            "hcin": hc_np,
            "idin": id_np,
            "xgidx": np.ascontiguousarray(xg),
        })
    res = run_bass_kernel_spmd(nc, in_maps, core_ids=list(range(8)))
    outp = np.empty((4, 2048, 2048), np.float32)
    for core in range(8):
        bi, hf = core // 2, core % 2
        outp[bi, hf * 1024:(hf + 1) * 1024, :] = res.results[core]["out"]
    return outp


# version 19
# speedup vs baseline: 1.0070x; 1.0070x over previous
import numpy as np
import ml_dtypes
import concourse.bass as bass
import concourse.mybir as mybir
from concourse.bass_utils import run_bass_kernel_spmd
from concourse.bass import IndirectOffsetOnAxis

F32 = mybir.dt.float32
BF16 = mybir.dt.bfloat16
I32 = mybir.dt.int32
AF = mybir.ActivationFunctionType
ALU = mybir.AluOpType
AXX = mybir.AxisListType.X

T = 2048
NCH = 2048
RQ = 1024
NM = 64
NDC = 16
HDIM = 8192
EPS = 1e-5
FWHT_SCALE = float(1.0 / np.sqrt(8192.0))
INV_T = float(1.0 / 2048.0)

OFF0 = 28608
A_OFF = OFF0
B_OFF = OFF0 + 131072
C_OFF = OFF0 + 163840
SM_OFF = C_OFF + 32768


def _host_prepare(bv, cols, rows, W1, W2, W3):
    cols = np.asarray(cols); rows = np.asarray(rows)
    order = np.argsort(cols, kind="stable")
    groups = []
    i = 0
    while i < len(order):
        b = int(order[i])
        if i + 1 < len(order) and cols[order[i + 1]] == cols[b]:
            groups.append((int(cols[b]), b, int(order[i + 1])))
            i += 2
        else:
            groups.append((int(cols[b]), b, None))
            i += 1
    G = len(groups)
    bvp = np.zeros((G, 128, 8, 128), np.float32)
    for g, (c, bA, bB) in enumerate(groups):
        for ph, b in enumerate((bA, bB)):
            if b is None:
                continue
            w = bv[b].transpose(2, 1, 0).reshape(8, 2, 64, 64)
            w = w.transpose(1, 2, 0, 3).reshape(128, 8, 64)
            bvp[g, :, :, ph * 64:(ph + 1) * 64] = w

    p_ = np.arange(128)
    m_ = np.arange(64)
    h_of = (p_[None, :] >> 1) * 128 + 2 * m_[:, None] + (p_[None, :] & 1)
    w1t = W1.reshape(HDIM, 16, 128)[h_of].transpose(0, 3, 2, 1)
    w3t = W3.reshape(HDIM, 16, 128)[h_of].transpose(0, 3, 2, 1)
    wcat = np.stack([w1t.reshape(64, 128, 2048), w3t.reshape(64, 128, 2048)],
                    axis=2)
    wcat = np.ascontiguousarray(wcat.reshape(64, 128, 4096))

    w2v = W2.T[h_of]
    w2v = w2v.reshape(16, 4, 128, 4, 512)
    w2t = np.ascontiguousarray(
        w2v.transpose(3, 0, 2, 1, 4).reshape(4, 16, 128, 2048))

    return groups, bvp, wcat, w2t


def _hada(n):
    h = np.array([[1.0]], np.float32)
    while h.shape[0] < n:
        h = np.block([[h, h], [h, -h]])
    return h


def _build(groups, rows, dbg=False, dbg2=False):
    G = len(groups)
    srcs = sorted({c for (c, _, _) in groups})
    NS = len(srcs)
    sidx_of = {c: i for i, c in enumerate(srcs)}
    gsidx = [sidx_of[c] for (c, _, _) in groups]
    last_g_of_sidx = {}
    for g, si in enumerate(gsidx):
        last_g_of_sidx[si] = g
    targets = []
    seen = set()
    for g, (c, bA, bB) in enumerate(groups):
        tg = []
        for ph, b in enumerate((bA, bB)):
            if b is None:
                continue
            r = int(rows[b])
            tg.append((ph, r, r not in seen))
            seen.add(r)
        targets.append(tg)
    written_halves = set(seen)

    last_g_of_tile = {}
    for g, tg in enumerate(targets):
        for (ph, r, first) in tg:
            last_g_of_tile[r // 2] = g
    touched = sorted(last_g_of_tile)
    untouchedT = [Ti for Ti in range(16) if Ti not in last_g_of_tile]
    copy_order = untouchedT + sorted(touched, key=lambda Ti: last_g_of_tile[Ti])
    copy_pos = {Ti: p for p, Ti in enumerate(copy_order)}
    acc_order = sorted(touched, key=lambda Ti: last_g_of_tile[Ti])
    acc_pos = {Ti: p for p, Ti in enumerate(acc_order)}

    nc = bass.Bass("TRN2", target_bir_lowering=False,
                   detect_race_conditions=False)

    xb = nc.dram_tensor("xb", [NCH, T], F32, kind="ExternalInput").ap()
    bvp = nc.dram_tensor("bvp", [G, 128, 8, 128], F32, kind="ExternalInput").ap()
    wcat = nc.dram_tensor("wcat", [64, 128, 4096], F32, kind="ExternalInput").ap()
    w2t = nc.dram_tensor("w2t", [4, 16, 128, 2048], F32, kind="ExternalInput").ap()
    hcin = nc.dram_tensor("hcin", [128, 128], BF16, kind="ExternalInput").ap()
    idin = nc.dram_tensor("idin", [128, 128], BF16, kind="ExternalInput").ap()
    xgidx = nc.dram_tensor("xgidx", [128, 8], I32, kind="ExternalInput").ap()
    x1full = nc.dram_tensor("x1full", [NCH, T], F32, kind="Internal").ap()
    out = nc.dram_tensor("out", [RQ, T], F32, kind="ExternalOutput").ap()
    if dbg2:
        d_x1 = nc.dram_tensor("d_x1", [NCH, T], F32, kind="ExternalOutput").ap()
        d_hid = nc.dram_tensor("d_hid", [128, 65536], BF16,
                               kind="ExternalOutput").ap()
        d_sm = nc.dram_tensor("d_sm", [128, 16], F32, kind="ExternalOutput").ap()
        d_xnt = nc.dram_tensor("d_xnt", [128, 16384], BF16,
                               kind="ExternalOutput").ap()
        d_sg = nc.dram_tensor("d_sg", [128, 1024], F32, kind="ExternalOutput").ap()
        d_wb = nc.dram_tensor("d_wb", [2, 128, 4096], BF16,
                              kind="ExternalOutput").ap()
        d_x1b = nc.dram_tensor("d_x1b", [NCH, T], F32, kind="ExternalOutput").ap()
        d_log = nc.dram_tensor("d_log", [128, 256], F32, kind="ExternalOutput").ap()
        d_ya = nc.dram_tensor("d_ya", [128, 32768], F32, kind="ExternalOutput").ap()
    if dbg:
        d_x1 = nc.dram_tensor("d_x1", [NCH, T], F32, kind="ExternalOutput").ap()
        d_xnt = nc.dram_tensor("d_xnt", [128, 16384], BF16,
                               kind="ExternalOutput").ap()
        d_hid = nc.dram_tensor("d_hid", [128, 65536], BF16,
                               kind="ExternalOutput").ap()
        d_blk = nc.dram_tensor("d_blk", [4, 128, 2064], BF16,
                               kind="ExternalOutput").ap()
        d_sm = nc.dram_tensor("d_sm", [128, 16], F32, kind="ExternalOutput").ap()

    at = nc.alloc_sbuf_tensor_at
    YT = at("YT", [128, 16 * 2048], F32, offset=A_OFF)
    HID = at("HID", [128, 64 * 1024], BF16, offset=A_OFF)
    xblk = [at(f"xblk{i}", [128, 2064], BF16, offset=B_OFF + i * 4160)
            for i in range(4)]
    XnT = at("XnT", [128, 16384], BF16, offset=B_OFF)
    x1ds2 = [at(f"x1ds2_{i}", [128, 4096], F32, offset=B_OFF + i * 16384)
             for i in range(2)]
    xsrc = [at(f"xsrc{i}", [128, 2048], F32, offset=C_OFF + i * 8192) for i in range(2)]
    xbt = [at(f"xbt{i}", [128, 2048], F32, offset=C_OFF + i * 8192) for i in range(2)]
    x1rt = [at(f"x1rt{i}", [128, 2048], F32, offset=C_OFF + i * 8192) for i in range(2)]
    x1rt.append(at("x1rt2", [128, 2048], F32, offset=A_OFF + 15 * 8192))
    wbuf = [at(f"wbuf{i}", [128, 4096], BF16, offset=C_OFF + i * 8192) for i in range(2)]
    lntmp = [at(f"lntmp{i}", [128, 2048], BF16, offset=C_OFF + 16384 + i * 4096)
             for i in range(2)]
    xn = [at(f"xn{i}", [128, 2048], BF16, offset=C_OFF + 16384 + i * 4096)
          for i in range(2)]
    sg = at("sg", [128, 1024], F32, offset=C_OFF + 16384)
    _w2off = [16384, 20480, 0, 4096]
    w2b = [at(f"w2b{i}", [128, 2048], BF16, offset=C_OFF + _w2off[i])
           for i in range(4)]
    junk = at("junk", [128, 2048], F32, offset=C_OFF + 24576)
    obuf = [at(f"obuf{i}", [128, 512], F32, offset=C_OFF + 8192 + i * 2048)
            for i in range(2)]
    bvw = [at(f"bvw{i}", [128, 1024], BF16, offset=B_OFF + 16640 + i * 2048)
           for i in range(2)]
    ssum = at("ssum", [128, 2], F32, offset=SM_OFF)
    ssq = at("ssq", [128, 2], F32, offset=SM_OFF + 32)
    smean = at("smean", [128, 2], F32, offset=SM_OFF + 64)
    sm2 = at("sm2", [128, 2], F32, offset=SM_OFF + 96)
    svar = at("svar", [128, 2], F32, offset=SM_OFF + 128)
    sstd = at("sstd", [128, 2], F32, offset=SM_OFF + 160)
    srstd = at("srstd", [128, 2], F32, offset=SM_OFF + 192)
    snmr = at("snmr", [128, 2], F32, offset=SM_OFF + 224)
    hc_sb = at("hc_sb", [128, 128], BF16, offset=SM_OFF + 256)
    id_sb = at("id_sb", [128, 128], BF16, offset=SM_OFF + 512)
    idx_sb = at("idx_sb", [128, 8], I32, offset=SM_OFF + 768)
    seps = at("seps", [128, 1], F32, offset=SM_OFF + 960)
    slog = at("slog", [128, 256], F32, offset=SM_OFF + 1024)

    pA = nc.alloc_psum_tensor("pA", [128, 2048], F32)
    pB = nc.alloc_psum_tensor("pB", [128, 2048], F32)
    pgrp = [pA, pB]
    ptx = [pA, pB]
    pg = [pA[:, 0:1024], pB[:, 0:1024]]
    pu = [pA[:, 1024:2048], pB[:, 1024:2048]]

    def pds(rt):
        base = pA if rt < 4 else pB
        return base[:, (rt % 4) * 512:(rt % 4) * 512 + 512]

    semnames = ["init", "xsrc0", "xsrc1", "bvw0", "bvw1", "acc", "var",
                "sqrt", "nmr", "lntmp", "xblk", "grpmm", "evac", "msk",
                "xcp", "gch", "sch", "x1f", "gat0", "gat1",
                "xn", "tx", "xnt", "wd0", "wd1", "gu", "silu", "mul", "hc",
                "hce", "bf", "w2q0", "w2q1", "w2q2", "w2q3", "gds0", "gds1", "gds2", "gat2", "dmm", "oev",
                "st0", "st1", "dbgx", "vch", "dmmf", "gg"]
    SEM = {n: nc.alloc_semaphore(f"s_{n}") for n in semnames}
    trace = {k: [] for k in ["sync", "gpsimd", "tensor", "scalar", "vector"]}

    def tile_ready(eng, key, Ti):
        W(eng, key, "xcp", 16)
        W(eng, key, "x1f", 16 * len(acc_order))

    def W(eng, key, sem, val):
        if val <= 0:
            return
        eng.wait_ge(SEM[sem], val)
        trace[key].append(("w", sem, val))

    def I(instr, key, sem, n=1):
        instr.then_inc(SEM[sem], n)
        trace[key].append(("i", sem, n))
        return instr

    blk_ctx = nc.Block()
    blk = blk_ctx.__enter__()

    def emit_sync(sy):
        k = "sync"
        I(sy.dma_start(hc_sb[:], hcin), k, "init", 16)
        I(sy.dma_start(id_sb[:], idin), k, "init", 16)
        I(sy.dma_start(idx_sb[:], xgidx), k, "init", 16)
        for si in range(NS):
            c = srcs[si]
            W(sy, k, "lntmp", si - 1)
            I(sy.dma_start(xsrc[si % 2][0:64, :], xb[c * 64:(c + 1) * 64, :]),
              k, f"xsrc{si % 2}", 16)
            I(sy.dma_start(xsrc[si % 2][64:128, :], xb[c * 64:(c + 1) * 64, :]),
              k, f"xsrc{si % 2}", 16)
        if dbg2:
            W(sy, k, "evac", G)
            I(sy.dma_start(d_ya, YT[:]), k, "dbgx", 16)
        if dbg:
            W(sy, k, "xblk", NS)
            for i in range(4):
                I(sy.dma_start(d_blk[i], xblk[i][:]), k, "dbgx", 16)
            W(sy, k, "xnt", 8)
            I(sy.dma_start(d_xnt, XnT[:]), k, "dbgx", 16)
        if dbg2:
            W(sy, k, "x1f", 16 * len(acc_order))
            I(sy.dma_start(d_x1b, x1full), k, "dbgx", 16)
            W(sy, k, "xnt", 8)
            I(sy.dma_start(d_xnt, XnT[:]), k, "dbgx", 16)
        for ds in range(4):
            for rt in range(8):
                ko = ds * 8 + rt
                W(sy, k, "oev", ko + 1)
                I(sy.dma_start(out[rt * 128:(rt + 1) * 128,
                                   ds * 512:(ds + 1) * 512],
                               obuf[ko % 2][:]),
                  k, f"st{ko % 2}", 16)
        W(sy, k, "st0", 16 * 16)
        W(sy, k, "st1", 16 * 16)
        if dbg2:
            I(sy.dma_start(d_x1, x1full), k, "dbgx", 16)
            I(sy.dma_start(d_hid, HID[:]), k, "dbgx", 16)
            I(sy.dma_start(d_sg, sg[:]), k, "dbgx", 16)
            I(sy.dma_start(d_log, slog[:]), k, "dbgx", 16)
            I(sy.dma_start(d_wb[0], wbuf[0][:]), k, "dbgx", 16)
            I(sy.dma_start(d_wb[1], wbuf[1][:]), k, "dbgx", 16)
            for j, tens in enumerate([ssum, ssq, smean, sm2, svar, sstd,
                                      srstd, snmr]):
                I(sy.dma_start(d_sm[:, j * 2:j * 2 + 2], tens[:]), k, "dbgx", 16)
            W(sy, k, "dbgx", 16 * 10)
        if dbg:
            I(sy.dma_start(d_x1, x1full), k, "dbgx", 16)
            I(sy.dma_start(d_hid, HID[:]), k, "dbgx", 16)
            for j, tens in enumerate([ssum, ssq, smean, sm2, svar, sstd,
                                      srstd, snmr]):
                I(sy.dma_start(d_sm[:, j * 2:j * 2 + 2], tens[:]), k, "dbgx", 16)
            W(sy, k, "dbgx", 16 * 15)

    def emit_gpsimd(g):
        k = "gpsimd"
        gcnt = [0]

        def ggate(instr):
            instr.then_inc(SEM["gch"], 1)
            trace[k].append(("i", "gch", 1))
            gcnt[0] += 1
            g.wait_ge(SEM["gch"], gcnt[0])
            trace[k].append(("w", "gch", gcnt[0]))

        for i in range(4):
            I(g.memset(xblk[i][:], 0.0), k, "msk", 1)
        ggate(g.memset(seps[:], EPS))

        done_var, done_nmr, done_cp = set(), set(), set()

        def gvar(i):
            if i in done_var or not (0 <= i < NTOT):
                return
            done_var.add(i)
            sp = i % 2
            W(g, k, "sqrt", i - 1)
            W(g, k, "acc", i + 1)
            ggate(g.tensor_mul(sm2[:, sp:sp + 1], ssum[:, sp:sp + 1],
                               ssum[:, sp:sp + 1]))
            ggate(g.tensor_sub(svar[:, sp:sp + 1], ssq[:, sp:sp + 1],
                               sm2[:, sp:sp + 1]))
            I(g.tensor_add(svar[:, sp:sp + 1], svar[:, sp:sp + 1],
                           seps[:, 0:1]), k, "var", 1)

        def gnmr(i):
            if i in done_nmr or not (0 <= i < NTOT):
                return
            done_nmr.add(i)
            sp = i % 2
            if i >= 2:
                if i - 2 < NS:
                    W(g, k, "lntmp", i - 1)
                else:
                    W(g, k, "xn", i - NS - 1)
            W(g, k, "sqrt", i + 1)
            I(g.tensor_mul(snmr[:, sp:sp + 1], ssum[:, sp:sp + 1],
                           srstd[:, sp:sp + 1]), k, "nmr", 1)

        def gcp(i):
            if i in done_cp or not (0 <= i < NS):
                return
            done_cp.add(i)
            sl = xblk[i % 4]
            W(g, k, "lntmp", i + 1)
            if i >= 4:
                W(g, k, "grpmm", last_g_of_sidx[i - 4] + 1)
            g.tensor_copy(sl[0:64, 15:2063], lntmp[i % 2][0:64, :])
            I(g.tensor_copy(sl[64:128, 14:2062], lntmp[i % 2][64:128, :]),
              k, "xblk", 1)

        def gstep(i):
            gvar(i)
            gnmr(i - 1)
            gcp(i - 2)

        acc_by_g = {}
        for Ti in acc_order:
            acc_by_g.setdefault(min(last_g_of_tile[Ti] + 5, G - 1), []).append(Ti)

        si_done = [0]
        for gi in range(G):
            tgt = min(NS, gsidx[gi] + 3)
            while si_done[0] < tgt:
                gstep(si_done[0])
                si_done[0] += 1
            for j in range(gsidx[gi] + 1):
                if j not in done_cp:
                    gvar(j)
                    gnmr(j)
                    gcp(j)
            W(g, k, "grpmm", gi - 1)
            I(g.dma_start(bvw[gi % 2][:], bvp[gi].rearrange("p t f -> p (t f)")),
              k, f"bvw{gi % 2}", 16)
            for Ti in acc_by_g.get(gi, []):
                W(g, k, "evac", gi + 1)
                W(g, k, "xcp", 16)
                I(g.dma_start(x1full[Ti * 128:(Ti + 1) * 128, :],
                              YT[:, Ti * 2048:(Ti + 1) * 2048],
                              accum_op=ALU.add),
                  k, "x1f", 16)
        while si_done[0] < NS:
            gstep(si_done[0])
            si_done[0] += 1
        gnmr(NS - 1)
        gcp(NS - 2)
        gcp(NS - 1)
        for rt in range(8):
            if rt == 0:
                W(g, k, "init", 48)
            W(g, k, "xn", rt - 2)
            tile_ready(g, k, rt)
            tile_ready(g, k, 8 + rt)
            if rt % 3 == 2:
                W(g, k, "x1f", 16 * len(acc_order))
            I(g.indirect_dma_start(x1rt[rt % 3][:], None, x1full,
                                   IndirectOffsetOnAxis(idx_sb[:, rt:rt + 1], 0)),
              k, f"gat{rt % 3}", 16)
            gstep(NS + rt)
        gnmr(NTOT - 1)
        for m in range(NM):
            if m < 2:
                W(g, k, "xn", 8)
            W(g, k, "gu", m - 1)
            I(g.dma_start(wbuf[m % 2][:], wcat[m]), k, f"wd{m % 2}", 16)
        for ds in range(4):
            if ds < 2:
                W(g, k, "gu", 64)
                if dbg:
                    W(g, k, "dbgx", 16 * 5)
                if dbg2:
                    W(g, k, "dbgx", 16 * 2)
            W(g, k, "oev", 8 * (ds - 1))
            for rt in range(8):
                I(g.indirect_dma_start(
                    x1ds2[ds % 2][:, rt * 512:(rt + 1) * 512],
                    None, x1full,
                    IndirectOffsetOnAxis(idx_sb[:, rt:rt + 1], 0),
                    element_offset=ds * 512),
                  k, f"gds{ds % 2}", 16)
            for gq in range(16):
                idx = ds * 16 + gq
                if idx < 4:
                    W(g, k, "mul", 64)
                W(g, k, "dmm", idx - 3)
                I(g.dma_start(w2b[idx % 4][:], w2t[ds, gq]),
                  k, f"w2q{idx % 4}", 16)

    def emit_tensor(t):
        k = "tensor"

        def mm(out_ap, lhsT_ap, rhs_ap, start, stop, skip_ldw=False):
            tmpl = t.matmul(out_ap, lhsT_ap, rhs_ap, start=start, stop=stop)
            if not skip_ldw:
                return tmpl
            ins = tmpl.ins
            newi = mybir.InstMatmult(
                name=ins.name, ins=list(ins.ins), outs=list(ins.outs),
                ldweights=False,
                replication_resolution=ins.replication_resolution,
                replication_shift_amnt=ins.replication_shift_amnt,
                replication_num_rows=ins.replication_num_rows,
                start_tensor_calc=ins.start_tensor_calc,
                stop_tensor_calc=ins.stop_tensor_calc,
                engine=ins.engine)
            insts = nc.cur_bb.bb.instructions
            for j in range(len(insts) - 1, -1, -1):
                if insts[j].name == ins.name:
                    insts[j] = newi
                    break
            nc.register_instruction(newi, overwrite=True)
            return bass.BassInstruction(newi)
        for gi in range(G):
            if gi == 0:
                W(t, k, "msk", 4)
            W(t, k, f"bvw{gi % 2}", 16 * (gi // 2 + 1))
            W(t, k, "xblk", gsidx[gi] + 1)
            W(t, k, "evac", gi - 1)
            sl = xblk[gsidx[gi] % 4]
            for tp in range(8):
                for ts in range(4):
                    ins = mm(pgrp[gi % 2][:, ts * 512:(ts + 1) * 512],
                             bvw[gi % 2][:, tp * 128:(tp + 1) * 128],
                             sl[:, 2 * tp + ts * 512:2 * tp + ts * 512 + 512],
                             start=(tp == 0), stop=(tp == 7), skip_ldw=(ts > 0))
            I(ins, k, "grpmm", 1)
        for rt in range(8):
            if rt < 2:
                W(t, k, "evac", G)
                W(t, k, "init", 48)
            W(t, k, "xn", rt + 1)
            W(t, k, "xnt", rt - 1)
            for dc in range(16):
                ins = t.matmul(ptx[rt % 2][:, dc * 128:(dc + 1) * 128],
                               xn[rt % 2][:, dc * 128:(dc + 1) * 128],
                               id_sb[:], start=True, stop=True)
            I(ins, k, "tx", 1)
        def emit_hc(m):
            W(t, k, "mul", m + 1)
            par = m % 2
            for rh in range(2):
                ins = mm(pu[par][:, rh * 512:(rh + 1) * 512],
                         hc_sb[:],
                         HID[:, m * 1024 + rh * 512:m * 1024 + rh * 512 + 512],
                         start=True, stop=True, skip_ldw=(rh == 1))
            I(ins, k, "hc", 1)

        for m in range(NM):
            if m == 0:
                W(t, k, "xnt", 8)
            W(t, k, f"wd{m % 2}", 16 * (m // 2 + 1))
            W(t, k, "hce", m - 1)
            W(t, k, "silu", m - 1)
            par = m % 2
            for dc in range(16):
                for rh in range(2):
                    ins = mm(pg[par][:, rh * 512:(rh + 1) * 512],
                             wbuf[m % 2][:, dc * 128:(dc + 1) * 128],
                             XnT[:, dc * 1024 + rh * 512:dc * 1024 + rh * 512 + 512],
                             start=(dc == 0), stop=(dc == 15), skip_ldw=(rh == 1))
            I(ins, k, "gg", 1)
            if m >= 1:
                emit_hc(m - 1)
            for dc in range(16):
                for rh in range(2):
                    ins = mm(pu[par][:, rh * 512:(rh + 1) * 512],
                             wbuf[m % 2][:, 2048 + dc * 128:2048 + (dc + 1) * 128],
                             XnT[:, dc * 1024 + rh * 512:dc * 1024 + rh * 512 + 512],
                             start=(dc == 0), stop=(dc == 15), skip_ldw=(rh == 1))
            I(ins, k, "gu", 1)
        emit_hc(NM - 1)
        W(t, k, "bf", 2)
        for ds in range(4):
            for gq in range(16):
                idx = ds * 16 + gq
                W(t, k, f"w2q{idx % 4}", 16 * (idx // 4 + 1))
                if gq == 0 and ds >= 1:
                    for rt in range(8):
                        W(t, k, "oev", (ds - 1) * 8 + rt + 1)
                for gi4 in range(4):
                    gg = gq * 4 + gi4
                    for rt in range(8):
                        ins = mm(pds(rt),
                                 HID[:, gg * 1024 + rt * 128:gg * 1024 + (rt + 1) * 128],
                                 w2b[idx % 4][:, gi4 * 512:(gi4 + 1) * 512],
                                 start=(gq == 0 and gi4 == 0),
                                 stop=(gq == 15 and gi4 == 3))
                        if gq == 15 and gi4 == 3 and rt < 7:
                            I(ins, k, "dmmf", 1)
                I(ins, k, "dmm", 1)

    NTOT = NS + 8

    def stat_src(i):
        if i < NS:
            return xsrc[i % 2], i % 2
        return x1rt[(i - NS) % 3], (i - NS) % 2

    def emit_scalar(s):
        k = "scalar"
        scnt = [0]

        def sgate(instr):
            instr.then_inc(SEM["sch"], 1)
            trace[k].append(("i", "sch", 1))
            scnt[0] += 1
            s.wait_ge(SEM["sch"], scnt[0])
            trace[k].append(("w", "sch", scnt[0]))

        I(s.dma_start(x1full, xb), k, "xcp", 16)

        def acc(i):
            buf, _ = stat_src(i)
            sp = i % 2
            W(s, k, "var", i - 1)
            if i < NS:
                W(s, k, f"xsrc{i % 2}", 32 * (i // 2 + 1))
            else:
                rt = i - NS
                W(s, k, f"gat{rt % 3}", 16 * (rt // 3 + 1))
            s.activation(junk[:], buf[:], AF.Square,
                         scale=0.02209708691207961,
                         accum_out=ssq[:, sp:sp + 1])
            I(s.activation(junk[:], buf[:], AF.Identity, scale=-INV_T,
                           accum_out=ssum[:, sp:sp + 1]), k, "acc", 1)

        def sqrt_(i):
            sp = i % 2
            W(s, k, "nmr", i - 1)
            W(s, k, "var", i + 1)
            sgate(s.activation(sstd[:, sp:sp + 1], svar[:, sp:sp + 1], AF.Ln))
            I(s.activation(srstd[:, sp:sp + 1], sstd[:, sp:sp + 1], AF.Exp,
                           scale=-0.5), k, "sqrt", 1)

        def norm(i):
            buf, bp = stat_src(i)
            sp = i % 2
            W(s, k, "nmr", i + 1)
            if i < NS:
                W(s, k, "xblk", i - 1)
                I(s.activation(lntmp[bp][:], buf[:], AF.Identity,
                               bias=snmr[:, sp:sp + 1],
                               scale=srstd[:, sp:sp + 1]), k, "lntmp", 1)
            else:
                rt = i - NS
                if rt < 2:
                    W(s, k, "xblk", NS)
                W(s, k, "tx", rt - 1)
                I(s.activation(xn[bp][:], buf[:], AF.Identity,
                               bias=snmr[:, sp:sp + 1],
                               scale=srstd[:, sp:sp + 1]), k, "xn", 1)

        done_sq, done_nm = set(), set()

        def sqrt_once(i):
            if 0 <= i < NTOT and i not in done_sq:
                done_sq.add(i)
                sqrt_(i)

        def norm_once(i):
            if 0 <= i < NTOT and i not in done_nm:
                done_nm.add(i)
                norm(i)

        for i in range(NTOT):
            if i == NS:
                sqrt_once(i - 1)
                norm_once(i - 1)
            acc(i)
            if i >= 1:
                sqrt_once(i - 1)
                norm_once(i - 1)
        sqrt_once(NTOT - 1)
        norm_once(NTOT - 1)
        for m in range(NM):
            W(s, k, "gg", m + 1)
            W(s, k, "mul", m)
            I(s.activation(sg[:], pg[m % 2][:, :], AF.Silu), k, "silu", 1)

    def emit_vector(v):
        k = "vector"
        vcnt = [0]

        def vgate(instr):
            instr.then_inc(SEM["vch"], 1)
            trace[k].append(("i", "vch", 1))
            vcnt[0] += 1
            v.wait_ge(SEM["vch"], vcnt[0])
            trace[k].append(("w", "vch", vcnt[0]))

        def evac(gi):
            W(v, k, "grpmm", gi + 1)
            for (ph, r, first) in targets[gi]:
                dst = YT[(r % 2) * 64:(r % 2) * 64 + 64,
                         (r // 2) * 2048:(r // 2 + 1) * 2048]
                src = pgrp[gi % 2][ph * 64:ph * 64 + 64, :]
                if first:
                    vgate(v.tensor_copy(dst, src))
                else:
                    vgate(v.tensor_tensor(dst, src, dst, ALU.add))
            I(v.engine_nop(), k, "evac", 1)

        def txe(rt):
            W(v, k, "tx", rt + 1)
            dst = XnT.rearrange("p (dc r) -> p dc r", dc=16, r=1024)[
                :, :, rt * 128:(rt + 1) * 128]
            I(v.tensor_copy(dst, ptx[rt % 2][:, :].rearrange(
                "p (dc rr) -> p dc rr", dc=16, rr=128)), k, "xnt", 1)

        for gi in range(G):
            evac(gi)
        for rt in range(8):
            txe(rt)
        import os as _os
        _nstage = 0 if _os.environ.get("K_NOBFLY") else 6

        def bfly_op(b, hoi, half):
            d = 1 << b
            r0 = half * 512
            sub = HID[:, hoi * 2 * d * 1024:(hoi + 1) * 2 * d * 1024]
            view = sub.rearrange("p (hp hl r) -> p hp hl r", hp=2, hl=d, r=1024)
            Aap = view[:, 0, :, r0:r0 + 512]
            Bap = view[:, 1, :, r0:r0 + 512]
            vgate(v.tensor_tensor(Bap, Aap, Bap, ALU.subtract))
            vgate(v.scalar_tensor_tensor(Aap, Aap, 2.0, Bap,
                                         ALU.mult, ALU.subtract))

        def bfly_ready(m):
            for b in range(_nstage):
                d = 1 << b
                if (m + 1) % (2 * d) == 0:
                    yield b, (m + 1) // (2 * d) - 1

        def hce(m):
            W(v, k, "hc", m + 1)
            I(v.tensor_scalar_mul(HID[:, m * 1024:(m + 1) * 1024],
                                  pu[m % 2][:, :], FWHT_SCALE), k, "hce", 1)

        for m in range(NM):
            if m >= 1:
                hce(m - 1)
                if m - 1 < NM - 1:
                    for b, hoi in bfly_ready(m - 1):
                        bfly_op(b, hoi, 0)
                        bfly_op(b, hoi, 1)
            W(v, k, "silu", m + 1)
            W(v, k, "gu", m + 1)
            I(v.tensor_mul(HID[:, m * 1024:(m + 1) * 1024], sg[:],
                           pu[m % 2][:, :]), k, "mul", 1)
        hce(NM - 1)
        for half in range(2):
            for b, hoi in bfly_ready(NM - 1):
                bfly_op(b, hoi, half)
            I(v.engine_nop(), k, "bf", 1)
        for ds in range(4):
            W(v, k, f"gds{ds % 2}", 128 * (ds // 2 + 1))
            for rt in range(8):
                ko = ds * 8 + rt
                if ko >= 2:
                    W(v, k, f"st{ko % 2}", 16 * (ko // 2))
                if rt < 7:
                    W(v, k, "dmmf", ds * 7 + rt + 1)
                else:
                    W(v, k, "dmm", ds * 16 + 16)
                I(v.tensor_tensor(obuf[ko % 2][:], pds(rt),
                                  x1ds2[ds % 2][:, rt * 512:(rt + 1) * 512],
                                  ALU.add), k, "oev", 1)

    blk.sync(emit_sync)
    blk.gpsimd(emit_gpsimd)
    blk.tensor(emit_tensor)
    blk.scalar(emit_scalar)
    blk.vector(emit_vector)
    blk_ctx.__exit__(None, None, None)

    _simulate(trace)
    return nc


def _simulate(trace):
    from collections import deque, defaultdict
    qs = {kk: deque(vv) for kk, vv in trace.items()}
    val = defaultdict(int)
    progress = True
    while progress:
        progress = False
        for kk, q in qs.items():
            while q:
                tok = q[0]
                if tok[0] == "w" and val[tok[1]] < tok[2]:
                    break
                if tok[0] == "i":
                    val[tok[1]] += tok[2]
                q.popleft()
                progress = True
    stuck = {kk: (len(q), q[0]) for kk, q in qs.items() if q}
    if stuck:
        raise RuntimeError(f"semaphore deadlock: {stuck}; sems={dict(val)}")


def _enable_ldw_opt():
    return


def kernel(L0_attn_x, L0_attn_block_values, L0_attn_cols, L0_attn_rows,
           L0_ffn_W1, L0_ffn_W2, L0_ffn_W3):
    _enable_ldw_opt()
    x = np.ascontiguousarray(np.asarray(L0_attn_x, dtype=np.float32))
    bv = np.ascontiguousarray(np.asarray(L0_attn_block_values, dtype=np.float32))
    cols = np.asarray(L0_attn_cols)
    rows = np.asarray(L0_attn_rows)
    W1 = np.ascontiguousarray(np.asarray(L0_ffn_W1, dtype=np.float32))
    W2 = np.ascontiguousarray(np.asarray(L0_ffn_W2, dtype=np.float32))
    W3 = np.ascontiguousarray(np.asarray(L0_ffn_W3, dtype=np.float32))

    groups, bvp, wcat, w2t = _host_prepare(bv, cols, rows, W1, W2, W3)
    nc = _build(groups, rows)

    hc_np = np.ascontiguousarray(
        np.kron(_hada(64), _hada(2)).astype(ml_dtypes.bfloat16))
    id_np = np.eye(128, dtype=ml_dtypes.bfloat16)

    in_maps = []
    for core in range(8):
        bi, hf = core // 2, core % 2
        xg = (1024 * hf + np.arange(8)[None, :] * 128
              + np.arange(128)[:, None]).astype(np.int32)
        in_maps.append({
            "xb": x[bi],
            "bvp": bvp,
            "wcat": wcat,
            "w2t": w2t,
            "hcin": hc_np,
            "idin": id_np,
            "xgidx": np.ascontiguousarray(xg),
        })
    res = run_bass_kernel_spmd(nc, in_maps, core_ids=list(range(8)))
    outp = np.empty((4, 2048, 2048), np.float32)
    for core in range(8):
        bi, hf = core // 2, core % 2
        outp[bi, hf * 1024:(hf + 1) * 1024, :] = res.results[core]["out"]
    return outp


# version 21 (final)
# speedup vs baseline: 1.0595x; 1.0522x over previous
import numpy as np
import ml_dtypes
import concourse.bass as bass
import concourse.mybir as mybir
from concourse.bass_utils import run_bass_kernel_spmd
from concourse.bass import IndirectOffsetOnAxis

F32 = mybir.dt.float32
BF16 = mybir.dt.bfloat16
I32 = mybir.dt.int32
AF = mybir.ActivationFunctionType
ALU = mybir.AluOpType
AXX = mybir.AxisListType.X

T = 2048
NCH = 2048
RQ = 1024
NM = 64
NDC = 16
HDIM = 8192
EPS = 1e-5
FWHT_SCALE = float(1.0 / np.sqrt(8192.0))
INV_T = float(1.0 / 2048.0)

OFF0 = 28608
A_OFF = OFF0
B_OFF = OFF0 + 131072
C_OFF = OFF0 + 163840
SM_OFF = C_OFF + 32768


def _host_prepare(bv, cols, rows, W1, W2, W3):
    cols = np.asarray(cols); rows = np.asarray(rows)
    order = np.argsort(cols, kind="stable")
    groups = []
    i = 0
    while i < len(order):
        b = int(order[i])
        if i + 1 < len(order) and cols[order[i + 1]] == cols[b]:
            groups.append((int(cols[b]), b, int(order[i + 1])))
            i += 2
        else:
            groups.append((int(cols[b]), b, None))
            i += 1
    G = len(groups)
    bvp = np.zeros((G, 128, 8, 128), np.float32)
    for g, (c, bA, bB) in enumerate(groups):
        for ph, b in enumerate((bA, bB)):
            if b is None:
                continue
            w = bv[b].transpose(2, 1, 0).reshape(8, 2, 64, 64)
            w = w.transpose(1, 2, 0, 3).reshape(128, 8, 64)
            bvp[g, :, :, ph * 64:(ph + 1) * 64] = w

    p_ = np.arange(128)
    m_ = np.arange(64)
    h_of = (p_[None, :] >> 1) * 128 + 2 * m_[:, None] + (p_[None, :] & 1)
    w1t = W1.reshape(HDIM, 16, 128)[h_of].transpose(0, 3, 2, 1)
    w3t = W3.reshape(HDIM, 16, 128)[h_of].transpose(0, 3, 2, 1)
    wcat = np.stack([w1t.reshape(64, 128, 2048), w3t.reshape(64, 128, 2048)],
                    axis=2)
    wcat = np.ascontiguousarray(wcat.reshape(64, 128, 4096))

    w2v = W2.T[h_of]
    w2v = w2v.reshape(16, 4, 128, 4, 512)
    w2t = np.ascontiguousarray(
        w2v.transpose(3, 0, 2, 1, 4).reshape(4, 16, 128, 2048))

    return groups, bvp, wcat, w2t


def _hada(n):
    h = np.array([[1.0]], np.float32)
    while h.shape[0] < n:
        h = np.block([[h, h], [h, -h]])
    return h


def _build(groups, rows, dbg=False, dbg2=False):
    G = len(groups)
    srcs = sorted({c for (c, _, _) in groups})
    NS = len(srcs)
    sidx_of = {c: i for i, c in enumerate(srcs)}
    gsidx = [sidx_of[c] for (c, _, _) in groups]
    last_g_of_sidx = {}
    for g, si in enumerate(gsidx):
        last_g_of_sidx[si] = g
    targets = []
    seen = set()
    for g, (c, bA, bB) in enumerate(groups):
        tg = []
        for ph, b in enumerate((bA, bB)):
            if b is None:
                continue
            r = int(rows[b])
            tg.append((ph, r, r not in seen))
            seen.add(r)
        targets.append(tg)
    written_halves = set(seen)

    last_g_of_tile = {}
    for g, tg in enumerate(targets):
        for (ph, r, first) in tg:
            last_g_of_tile[r // 2] = g
    touched = sorted(last_g_of_tile)
    untouchedT = [Ti for Ti in range(16) if Ti not in last_g_of_tile]
    copy_order = untouchedT + sorted(touched, key=lambda Ti: last_g_of_tile[Ti])
    copy_pos = {Ti: p for p, Ti in enumerate(copy_order)}
    acc_order = sorted(touched, key=lambda Ti: last_g_of_tile[Ti])
    acc_pos = {Ti: p for p, Ti in enumerate(acc_order)}

    nc = bass.Bass("TRN2", target_bir_lowering=False,
                   detect_race_conditions=False)

    xb = nc.dram_tensor("xb", [NCH, T], F32, kind="ExternalInput").ap()
    bvp = nc.dram_tensor("bvp", [G, 128, 8, 128], F32, kind="ExternalInput").ap()
    wcat = nc.dram_tensor("wcat", [64, 128, 4096], F32, kind="ExternalInput").ap()
    w2t = nc.dram_tensor("w2t", [4, 16, 128, 2048], F32, kind="ExternalInput").ap()
    hcin = nc.dram_tensor("hcin", [128, 128], BF16, kind="ExternalInput").ap()
    idin = nc.dram_tensor("idin", [128, 128], BF16, kind="ExternalInput").ap()
    xgidx = nc.dram_tensor("xgidx", [128, 8], I32, kind="ExternalInput").ap()
    x1full = nc.dram_tensor("x1full", [NCH, T], F32, kind="Internal").ap()
    out = nc.dram_tensor("out", [RQ, T], F32, kind="ExternalOutput").ap()
    if dbg2:
        d_x1 = nc.dram_tensor("d_x1", [NCH, T], F32, kind="ExternalOutput").ap()
        d_hid = nc.dram_tensor("d_hid", [128, 65536], BF16,
                               kind="ExternalOutput").ap()
        d_sm = nc.dram_tensor("d_sm", [128, 16], F32, kind="ExternalOutput").ap()
        d_xnt = nc.dram_tensor("d_xnt", [128, 16384], BF16,
                               kind="ExternalOutput").ap()
        d_sg = nc.dram_tensor("d_sg", [128, 1024], F32, kind="ExternalOutput").ap()
        d_wb = nc.dram_tensor("d_wb", [2, 128, 4096], BF16,
                              kind="ExternalOutput").ap()
        d_x1b = nc.dram_tensor("d_x1b", [NCH, T], F32, kind="ExternalOutput").ap()
        d_log = nc.dram_tensor("d_log", [128, 256], F32, kind="ExternalOutput").ap()
        d_ya = nc.dram_tensor("d_ya", [128, 32768], F32, kind="ExternalOutput").ap()
    if dbg:
        d_x1 = nc.dram_tensor("d_x1", [NCH, T], F32, kind="ExternalOutput").ap()
        d_xnt = nc.dram_tensor("d_xnt", [128, 16384], BF16,
                               kind="ExternalOutput").ap()
        d_hid = nc.dram_tensor("d_hid", [128, 65536], BF16,
                               kind="ExternalOutput").ap()
        d_blk = nc.dram_tensor("d_blk", [4, 128, 2064], BF16,
                               kind="ExternalOutput").ap()
        d_sm = nc.dram_tensor("d_sm", [128, 16], F32, kind="ExternalOutput").ap()

    at = nc.alloc_sbuf_tensor_at
    YT = at("YT", [128, 16 * 2048], F32, offset=A_OFF)
    HID = at("HID", [128, 64 * 1024], BF16, offset=A_OFF)
    xblk = [at(f"xblk{i}", [128, 2064], BF16, offset=B_OFF + i * 4160)
            for i in range(4)]
    XnT = at("XnT", [128, 16384], BF16, offset=B_OFF)
    x1ds2 = [at(f"x1ds2_{i}", [128, 4096], F32, offset=B_OFF + i * 16384)
             for i in range(2)]
    xsrc = [at(f"xsrc{i}", [128, 2048], F32, offset=C_OFF + i * 8192) for i in range(2)]
    xbt = [at(f"xbt{i}", [128, 2048], F32, offset=C_OFF + i * 8192) for i in range(2)]
    x1rt = [at(f"x1rt{i}", [128, 2048], F32, offset=C_OFF + i * 8192) for i in range(2)]
    x1rt.append(at("x1rt2", [128, 2048], F32, offset=A_OFF + 15 * 8192))
    wbuf = [at(f"wbuf{i}", [128, 4096], BF16, offset=C_OFF + i * 8192) for i in range(2)]
    lntmp = [at(f"lntmp{i}", [128, 2048], BF16, offset=C_OFF + 16384 + i * 4096)
             for i in range(2)]
    xn = [at(f"xn{i}", [128, 2048], BF16, offset=C_OFF + 16384 + i * 4096)
          for i in range(2)]
    sg = at("sg", [128, 1024], F32, offset=C_OFF + 16384)
    _w2off = [16384, 20480, 0, 4096]
    w2b = [at(f"w2b{i}", [128, 2048], BF16, offset=C_OFF + _w2off[i])
           for i in range(4)]
    junk = at("junk", [128, 2048], F32, offset=C_OFF + 24576)
    obuf = [at(f"obuf{i}", [128, 512], F32, offset=C_OFF + 8192 + i * 2048)
            for i in range(2)]
    bvw = [at(f"bvw{i}", [128, 1024], BF16, offset=B_OFF + 16640 + i * 2048)
           for i in range(2)]
    ssum = at("ssum", [128, 2], F32, offset=SM_OFF)
    ssq = at("ssq", [128, 2], F32, offset=SM_OFF + 32)
    smean = at("smean", [128, 2], F32, offset=SM_OFF + 64)
    sm2 = at("sm2", [128, 2], F32, offset=SM_OFF + 96)
    svar = at("svar", [128, 2], F32, offset=SM_OFF + 128)
    sstd = at("sstd", [128, 2], F32, offset=SM_OFF + 160)
    srstd = at("srstd", [128, 2], F32, offset=SM_OFF + 192)
    snmr = at("snmr", [128, 2], F32, offset=SM_OFF + 224)
    hc_sb = at("hc_sb", [128, 128], BF16, offset=SM_OFF + 256)
    id_sb = at("id_sb", [128, 128], BF16, offset=SM_OFF + 512)
    idx_sb = at("idx_sb", [128, 8], I32, offset=SM_OFF + 768)
    seps = at("seps", [128, 1], F32, offset=SM_OFF + 960)
    slog = at("slog", [128, 256], F32, offset=SM_OFF + 1024)

    pA = nc.alloc_psum_tensor("pA", [128, 2048], F32)
    pB = nc.alloc_psum_tensor("pB", [128, 2048], F32)
    pgrp = [pA, pB]
    ptx = [pA, pB]
    pg = [pA[:, 0:1024], pB[:, 0:1024]]
    pu = [pA[:, 1024:2048], pB[:, 1024:2048]]

    def pds(rt):
        base = pA if rt < 4 else pB
        return base[:, (rt % 4) * 512:(rt % 4) * 512 + 512]

    semnames = ["init", "xsrc0", "xsrc1", "bvw0", "bvw1", "acc", "var",
                "sqrt", "nmr", "lntmp", "xblk", "grpmm", "evac", "msk",
                "xcp", "gch", "sch", "x1f", "gat0", "gat1",
                "xn", "tx", "xnt", "wd0", "wd1", "gu", "silu", "mul", "hc",
                "hce", "bf", "w2q0", "w2q1", "w2q2", "w2q3", "gds0", "gds1", "gds2", "gat2", "dmm", "oev",
                "st0", "st1", "dbgx", "vch", "dmmf", "gg"]
    SEM = {n: nc.alloc_semaphore(f"s_{n}") for n in semnames}
    trace = {k: [] for k in ["sync", "gpsimd", "tensor", "scalar", "vector"]}

    def tile_ready(eng, key, Ti):
        W(eng, key, "xcp", 16)
        W(eng, key, "x1f", 16 * len(acc_order))

    def W(eng, key, sem, val):
        if val <= 0:
            return
        eng.wait_ge(SEM[sem], val)
        trace[key].append(("w", sem, val))

    def I(instr, key, sem, n=1):
        instr.then_inc(SEM[sem], n)
        trace[key].append(("i", sem, n))
        return instr

    blk_ctx = nc.Block()
    blk = blk_ctx.__enter__()

    def emit_sync(sy):
        k = "sync"
        I(sy.dma_start(hc_sb[:], hcin), k, "init", 16)
        I(sy.dma_start(id_sb[:], idin), k, "init", 16)
        I(sy.dma_start(idx_sb[:], xgidx), k, "init", 16)
        for si in range(NS):
            c = srcs[si]
            W(sy, k, "lntmp", si - 1)
            I(sy.dma_start(xsrc[si % 2][0:64, :], xb[c * 64:(c + 1) * 64, :]),
              k, f"xsrc{si % 2}", 16)
            I(sy.dma_start(xsrc[si % 2][64:128, :], xb[c * 64:(c + 1) * 64, :]),
              k, f"xsrc{si % 2}", 16)
        if dbg2:
            W(sy, k, "evac", G)
            I(sy.dma_start(d_ya, YT[:]), k, "dbgx", 16)
        if dbg:
            W(sy, k, "xblk", NS)
            for i in range(4):
                I(sy.dma_start(d_blk[i], xblk[i][:]), k, "dbgx", 16)
            W(sy, k, "xnt", 8)
            I(sy.dma_start(d_xnt, XnT[:]), k, "dbgx", 16)
        if dbg2:
            W(sy, k, "x1f", 16 * len(acc_order))
            I(sy.dma_start(d_x1b, x1full), k, "dbgx", 16)
            W(sy, k, "xnt", 8)
            I(sy.dma_start(d_xnt, XnT[:]), k, "dbgx", 16)
        for ds in range(4):
            for rt in range(8):
                ko = ds * 8 + rt
                W(sy, k, "oev", ko + 1)
                I(sy.dma_start(out[rt * 128:(rt + 1) * 128,
                                   ds * 512:(ds + 1) * 512],
                               obuf[ko % 2][:]),
                  k, f"st{ko % 2}", 16)
        W(sy, k, "st0", 16 * 16)
        W(sy, k, "st1", 16 * 16)
        if dbg2:
            I(sy.dma_start(d_x1, x1full), k, "dbgx", 16)
            I(sy.dma_start(d_hid, HID[:]), k, "dbgx", 16)
            I(sy.dma_start(d_sg, sg[:]), k, "dbgx", 16)
            I(sy.dma_start(d_log, slog[:]), k, "dbgx", 16)
            I(sy.dma_start(d_wb[0], wbuf[0][:]), k, "dbgx", 16)
            I(sy.dma_start(d_wb[1], wbuf[1][:]), k, "dbgx", 16)
            for j, tens in enumerate([ssum, ssq, smean, sm2, svar, sstd,
                                      srstd, snmr]):
                I(sy.dma_start(d_sm[:, j * 2:j * 2 + 2], tens[:]), k, "dbgx", 16)
            W(sy, k, "dbgx", 16 * 10)
        if dbg:
            I(sy.dma_start(d_x1, x1full), k, "dbgx", 16)
            I(sy.dma_start(d_hid, HID[:]), k, "dbgx", 16)
            for j, tens in enumerate([ssum, ssq, smean, sm2, svar, sstd,
                                      srstd, snmr]):
                I(sy.dma_start(d_sm[:, j * 2:j * 2 + 2], tens[:]), k, "dbgx", 16)
            W(sy, k, "dbgx", 16 * 15)

    def emit_gpsimd(g):
        k = "gpsimd"
        gcnt = [0]

        def ggate(instr):
            instr.then_inc(SEM["gch"], 1)
            trace[k].append(("i", "gch", 1))
            gcnt[0] += 1
            g.wait_ge(SEM["gch"], gcnt[0])
            trace[k].append(("w", "gch", gcnt[0]))

        for i in range(4):
            I(g.memset(xblk[i][:], 0.0), k, "msk", 1)
        ggate(g.memset(seps[:], EPS))

        done_var, done_nmr, done_cp = set(), set(), set()

        def gvar(i):
            if i in done_var or not (0 <= i < NTOT):
                return
            done_var.add(i)
            sp = i % 2
            W(g, k, "sqrt", i - 1)
            W(g, k, "acc", i + 1)
            ggate(g.tensor_mul(sm2[:, sp:sp + 1], ssum[:, sp:sp + 1],
                               ssum[:, sp:sp + 1]))
            ggate(g.tensor_sub(svar[:, sp:sp + 1], ssq[:, sp:sp + 1],
                               sm2[:, sp:sp + 1]))
            I(g.tensor_add(svar[:, sp:sp + 1], svar[:, sp:sp + 1],
                           seps[:, 0:1]), k, "var", 1)

        def gnmr(i):
            if i in done_nmr or not (0 <= i < NTOT):
                return
            done_nmr.add(i)
            sp = i % 2
            if i >= 2:
                if i - 2 < NS:
                    W(g, k, "lntmp", i - 1)
                else:
                    W(g, k, "xn", i - NS - 1)
            W(g, k, "sqrt", i + 1)
            I(g.tensor_mul(snmr[:, sp:sp + 1], ssum[:, sp:sp + 1],
                           srstd[:, sp:sp + 1]), k, "nmr", 1)

        def gcp(i):
            if i in done_cp or not (0 <= i < NS):
                return
            done_cp.add(i)
            sl = xblk[i % 4]
            W(g, k, "lntmp", i + 1)
            if i >= 4:
                W(g, k, "grpmm", last_g_of_sidx[i - 4] + 1)
            g.tensor_copy(sl[0:64, 15:2063], lntmp[i % 2][0:64, :])
            I(g.tensor_copy(sl[64:128, 14:2062], lntmp[i % 2][64:128, :]),
              k, "xblk", 1)

        def gstep(i):
            gvar(i)
            gnmr(i - 1)
            gcp(i - 2)

        acc_by_g = {}

        si_done = [0]
        for gi in range(G):
            tgt = min(NS, gsidx[gi] + 3)
            while si_done[0] < tgt:
                gstep(si_done[0])
                si_done[0] += 1
            for j in range(gsidx[gi] + 1):
                if j not in done_cp:
                    gvar(j)
                    gnmr(j)
                    gcp(j)
            W(g, k, "grpmm", gi - 1)
            I(g.dma_start(bvw[gi % 2][:], bvp[gi].rearrange("p t f -> p (t f)")),
              k, f"bvw{gi % 2}", 16)
        for Ti in acc_order:
            W(g, k, "evac", last_g_of_tile[Ti] + 1)
            W(g, k, "xcp", 16)
            I(g.dma_start(x1full[Ti * 128:(Ti + 1) * 128, :],
                          YT[:, Ti * 2048:(Ti + 1) * 2048],
                          accum_op=ALU.add),
              k, "x1f", 16)
        while si_done[0] < NS:
            gstep(si_done[0])
            si_done[0] += 1
        gnmr(NS - 1)
        gcp(NS - 2)
        gcp(NS - 1)
        for rt in range(8):
            if rt == 0:
                W(g, k, "init", 48)
            W(g, k, "xn", rt - 2)
            tile_ready(g, k, rt)
            tile_ready(g, k, 8 + rt)
            if rt % 3 == 2:
                W(g, k, "x1f", 16 * len(acc_order))
            I(g.indirect_dma_start(x1rt[rt % 3][:], None, x1full,
                                   IndirectOffsetOnAxis(idx_sb[:, rt:rt + 1], 0)),
              k, f"gat{rt % 3}", 16)
            gstep(NS + rt)
        gnmr(NTOT - 1)
        for m in range(NM):
            if m < 2:
                W(g, k, "xn", 8)
            W(g, k, "gu", m - 1)
            I(g.dma_start(wbuf[m % 2][:], wcat[m]), k, f"wd{m % 2}", 16)
        for ds in range(4):
            if ds < 2:
                W(g, k, "gu", 64)
                if dbg:
                    W(g, k, "dbgx", 16 * 5)
                if dbg2:
                    W(g, k, "dbgx", 16 * 2)
            W(g, k, "oev", 8 * (ds - 1))
            for rt in range(8):
                I(g.indirect_dma_start(
                    x1ds2[ds % 2][:, rt * 512:(rt + 1) * 512],
                    None, x1full,
                    IndirectOffsetOnAxis(idx_sb[:, rt:rt + 1], 0),
                    element_offset=ds * 512),
                  k, f"gds{ds % 2}", 16)
            for gq in range(16):
                idx = ds * 16 + gq
                if idx < 4:
                    W(g, k, "mul", 64)
                W(g, k, "dmm", idx - 3)
                I(g.dma_start(w2b[idx % 4][:], w2t[ds, gq]),
                  k, f"w2q{idx % 4}", 16)

    def emit_tensor(t):
        k = "tensor"

        def mm(out_ap, lhsT_ap, rhs_ap, start, stop, skip_ldw=False):
            tmpl = t.matmul(out_ap, lhsT_ap, rhs_ap, start=start, stop=stop)
            if not skip_ldw:
                return tmpl
            ins = tmpl.ins
            newi = mybir.InstMatmult(
                name=ins.name, ins=list(ins.ins), outs=list(ins.outs),
                ldweights=False,
                replication_resolution=ins.replication_resolution,
                replication_shift_amnt=ins.replication_shift_amnt,
                replication_num_rows=ins.replication_num_rows,
                start_tensor_calc=ins.start_tensor_calc,
                stop_tensor_calc=ins.stop_tensor_calc,
                engine=ins.engine)
            insts = nc.cur_bb.bb.instructions
            for j in range(len(insts) - 1, -1, -1):
                if insts[j].name == ins.name:
                    insts[j] = newi
                    break
            nc.register_instruction(newi, overwrite=True)
            return bass.BassInstruction(newi)
        for gi in range(G):
            if gi == 0:
                W(t, k, "msk", 4)
            W(t, k, f"bvw{gi % 2}", 16 * (gi // 2 + 1))
            W(t, k, "xblk", gsidx[gi] + 1)
            W(t, k, "evac", gi - 1)
            sl = xblk[gsidx[gi] % 4]
            for tp in range(8):
                for ts in range(4):
                    ins = mm(pgrp[gi % 2][:, ts * 512:(ts + 1) * 512],
                             bvw[gi % 2][:, tp * 128:(tp + 1) * 128],
                             sl[:, 2 * tp + ts * 512:2 * tp + ts * 512 + 512],
                             start=(tp == 0), stop=(tp == 7), skip_ldw=(ts > 0))
            I(ins, k, "grpmm", 1)
        for rt in range(8):
            if rt < 2:
                W(t, k, "evac", G)
                W(t, k, "init", 48)
            W(t, k, "xn", rt + 1)
            W(t, k, "xnt", rt - 1)
            for dc in range(16):
                ins = t.matmul(ptx[rt % 2][:, dc * 128:(dc + 1) * 128],
                               xn[rt % 2][:, dc * 128:(dc + 1) * 128],
                               id_sb[:], start=True, stop=True)
            I(ins, k, "tx", 1)
        def emit_hc(m):
            W(t, k, "mul", m + 1)
            par = m % 2
            for rh in range(2):
                ins = mm(pu[par][:, rh * 512:(rh + 1) * 512],
                         hc_sb[:],
                         HID[:, m * 1024 + rh * 512:m * 1024 + rh * 512 + 512],
                         start=True, stop=True, skip_ldw=(rh == 1))
            I(ins, k, "hc", 1)

        for m in range(NM):
            if m == 0:
                W(t, k, "xnt", 8)
            W(t, k, f"wd{m % 2}", 16 * (m // 2 + 1))
            W(t, k, "hce", m - 1)
            W(t, k, "silu", m - 1)
            par = m % 2
            for dc in range(16):
                for rh in range(2):
                    ins = mm(pg[par][:, rh * 512:(rh + 1) * 512],
                             wbuf[m % 2][:, dc * 128:(dc + 1) * 128],
                             XnT[:, dc * 1024 + rh * 512:dc * 1024 + rh * 512 + 512],
                             start=(dc == 0), stop=(dc == 15), skip_ldw=(rh == 1))
            I(ins, k, "gg", 1)
            if m >= 1:
                emit_hc(m - 1)
            for dc in range(16):
                for rh in range(2):
                    ins = mm(pu[par][:, rh * 512:(rh + 1) * 512],
                             wbuf[m % 2][:, 2048 + dc * 128:2048 + (dc + 1) * 128],
                             XnT[:, dc * 1024 + rh * 512:dc * 1024 + rh * 512 + 512],
                             start=(dc == 0), stop=(dc == 15), skip_ldw=(rh == 1))
            I(ins, k, "gu", 1)
        emit_hc(NM - 1)
        W(t, k, "bf", 2)
        for ds in range(4):
            for gq in range(16):
                idx = ds * 16 + gq
                W(t, k, f"w2q{idx % 4}", 16 * (idx // 4 + 1))
                if gq == 0 and ds >= 1:
                    for rt in range(8):
                        W(t, k, "oev", (ds - 1) * 8 + rt + 1)
                for gi4 in range(4):
                    gg = gq * 4 + gi4
                    for rt in range(8):
                        ins = mm(pds(rt),
                                 HID[:, gg * 1024 + rt * 128:gg * 1024 + (rt + 1) * 128],
                                 w2b[idx % 4][:, gi4 * 512:(gi4 + 1) * 512],
                                 start=(gq == 0 and gi4 == 0),
                                 stop=(gq == 15 and gi4 == 3))
                        if gq == 15 and gi4 == 3 and rt < 7:
                            I(ins, k, "dmmf", 1)
                I(ins, k, "dmm", 1)

    NTOT = NS + 8

    def stat_src(i):
        if i < NS:
            return xsrc[i % 2], i % 2
        return x1rt[(i - NS) % 3], (i - NS) % 2

    def emit_scalar(s):
        k = "scalar"
        scnt = [0]

        def sgate(instr):
            instr.then_inc(SEM["sch"], 1)
            trace[k].append(("i", "sch", 1))
            scnt[0] += 1
            s.wait_ge(SEM["sch"], scnt[0])
            trace[k].append(("w", "sch", scnt[0]))

        I(s.dma_start(x1full, xb), k, "xcp", 16)

        def acc(i):
            buf, _ = stat_src(i)
            sp = i % 2
            W(s, k, "var", i - 1)
            if i < NS:
                W(s, k, f"xsrc{i % 2}", 32 * (i // 2 + 1))
            else:
                rt = i - NS
                W(s, k, f"gat{rt % 3}", 16 * (rt // 3 + 1))
            s.activation(junk[:], buf[:], AF.Square,
                         scale=0.02209708691207961,
                         accum_out=ssq[:, sp:sp + 1])
            I(s.activation(junk[:], buf[:], AF.Identity, scale=-INV_T,
                           accum_out=ssum[:, sp:sp + 1]), k, "acc", 1)

        def sqrt_(i):
            sp = i % 2
            W(s, k, "nmr", i - 1)
            W(s, k, "var", i + 1)
            sgate(s.activation(sstd[:, sp:sp + 1], svar[:, sp:sp + 1], AF.Ln))
            I(s.activation(srstd[:, sp:sp + 1], sstd[:, sp:sp + 1], AF.Exp,
                           scale=-0.5), k, "sqrt", 1)

        def norm(i):
            buf, bp = stat_src(i)
            sp = i % 2
            W(s, k, "nmr", i + 1)
            if i < NS:
                W(s, k, "xblk", i - 1)
                I(s.activation(lntmp[bp][:], buf[:], AF.Identity,
                               bias=snmr[:, sp:sp + 1],
                               scale=srstd[:, sp:sp + 1]), k, "lntmp", 1)
            else:
                rt = i - NS
                if rt < 2:
                    W(s, k, "xblk", NS)
                W(s, k, "tx", rt - 1)
                I(s.activation(xn[bp][:], buf[:], AF.Identity,
                               bias=snmr[:, sp:sp + 1],
                               scale=srstd[:, sp:sp + 1]), k, "xn", 1)

        done_sq, done_nm = set(), set()

        def sqrt_once(i):
            if 0 <= i < NTOT and i not in done_sq:
                done_sq.add(i)
                sqrt_(i)

        def norm_once(i):
            if 0 <= i < NTOT and i not in done_nm:
                done_nm.add(i)
                norm(i)

        for i in range(NTOT):
            if i == NS:
                sqrt_once(i - 1)
                norm_once(i - 1)
            acc(i)
            if i >= 1:
                sqrt_once(i - 1)
                norm_once(i - 1)
        sqrt_once(NTOT - 1)
        norm_once(NTOT - 1)
        for m in range(NM):
            W(s, k, "gg", m + 1)
            W(s, k, "mul", m)
            I(s.activation(sg[:], pg[m % 2][:, :], AF.Silu), k, "silu", 1)

    def emit_vector(v):
        k = "vector"
        vcnt = [0]

        def vgate(instr):
            instr.then_inc(SEM["vch"], 1)
            trace[k].append(("i", "vch", 1))
            vcnt[0] += 1
            v.wait_ge(SEM["vch"], vcnt[0])
            trace[k].append(("w", "vch", vcnt[0]))

        def evac(gi):
            W(v, k, "grpmm", gi + 1)
            for (ph, r, first) in targets[gi]:
                dst = YT[(r % 2) * 64:(r % 2) * 64 + 64,
                         (r // 2) * 2048:(r // 2 + 1) * 2048]
                src = pgrp[gi % 2][ph * 64:ph * 64 + 64, :]
                if first:
                    vgate(v.tensor_copy(dst, src))
                else:
                    vgate(v.tensor_tensor(dst, src, dst, ALU.add))
            I(v.engine_nop(), k, "evac", 1)

        def txe(rt):
            W(v, k, "tx", rt + 1)
            dst = XnT.rearrange("p (dc r) -> p dc r", dc=16, r=1024)[
                :, :, rt * 128:(rt + 1) * 128]
            I(v.tensor_copy(dst, ptx[rt % 2][:, :].rearrange(
                "p (dc rr) -> p dc rr", dc=16, rr=128)), k, "xnt", 1)

        for gi in range(G):
            evac(gi)
        for rt in range(8):
            txe(rt)
        import os as _os
        _nstage = 6

        def bfly_op(b, hoi, half):
            d = 1 << b
            r0 = half * 512
            sub = HID[:, hoi * 2 * d * 1024:(hoi + 1) * 2 * d * 1024]
            view = sub.rearrange("p (hp hl r) -> p hp hl r", hp=2, hl=d, r=1024)
            Aap = view[:, 0, :, r0:r0 + 512]
            Bap = view[:, 1, :, r0:r0 + 512]
            vgate(v.tensor_tensor(Bap, Aap, Bap, ALU.subtract))
            vgate(v.scalar_tensor_tensor(Aap, Aap, 2.0, Bap,
                                         ALU.mult, ALU.subtract))

        def bfly_ready(m):
            for b in range(_nstage):
                d = 1 << b
                if (m + 1) % (2 * d) == 0:
                    yield b, (m + 1) // (2 * d) - 1

        def hce(m):
            W(v, k, "hc", m + 1)
            I(v.tensor_scalar_mul(HID[:, m * 1024:(m + 1) * 1024],
                                  pu[m % 2][:, :], FWHT_SCALE), k, "hce", 1)

        for m in range(NM):
            if m >= 1:
                hce(m - 1)
                if m - 1 < NM - 1:
                    for b, hoi in bfly_ready(m - 1):
                        bfly_op(b, hoi, 0)
                        bfly_op(b, hoi, 1)
            W(v, k, "silu", m + 1)
            W(v, k, "gu", m + 1)
            I(v.tensor_mul(HID[:, m * 1024:(m + 1) * 1024], sg[:],
                           pu[m % 2][:, :]), k, "mul", 1)
        hce(NM - 1)
        for half in range(2):
            for b, hoi in bfly_ready(NM - 1):
                bfly_op(b, hoi, half)
            I(v.engine_nop(), k, "bf", 1)
        for ds in range(4):
            W(v, k, f"gds{ds % 2}", 128 * (ds // 2 + 1))
            for rt in range(8):
                ko = ds * 8 + rt
                if ko >= 2:
                    W(v, k, f"st{ko % 2}", 16 * (ko // 2))
                if rt < 7:
                    W(v, k, "dmmf", ds * 7 + rt + 1)
                else:
                    W(v, k, "dmm", ds * 16 + 16)
                I(v.tensor_tensor(obuf[ko % 2][:], pds(rt),
                                  x1ds2[ds % 2][:, rt * 512:(rt + 1) * 512],
                                  ALU.add), k, "oev", 1)

    blk.sync(emit_sync)
    blk.gpsimd(emit_gpsimd)
    blk.tensor(emit_tensor)
    blk.scalar(emit_scalar)
    blk.vector(emit_vector)
    blk_ctx.__exit__(None, None, None)

    _simulate(trace)
    return nc


def _simulate(trace):
    from collections import deque, defaultdict
    qs = {kk: deque(vv) for kk, vv in trace.items()}
    val = defaultdict(int)
    progress = True
    while progress:
        progress = False
        for kk, q in qs.items():
            while q:
                tok = q[0]
                if tok[0] == "w" and val[tok[1]] < tok[2]:
                    break
                if tok[0] == "i":
                    val[tok[1]] += tok[2]
                q.popleft()
                progress = True
    stuck = {kk: (len(q), q[0]) for kk, q in qs.items() if q}
    if stuck:
        raise RuntimeError(f"semaphore deadlock: {stuck}; sems={dict(val)}")


def _enable_ldw_opt():
    return


def kernel(L0_attn_x, L0_attn_block_values, L0_attn_cols, L0_attn_rows,
           L0_ffn_W1, L0_ffn_W2, L0_ffn_W3):
    _enable_ldw_opt()
    x = np.ascontiguousarray(np.asarray(L0_attn_x, dtype=np.float32))
    bv = np.ascontiguousarray(np.asarray(L0_attn_block_values, dtype=np.float32))
    cols = np.asarray(L0_attn_cols)
    rows = np.asarray(L0_attn_rows)
    W1 = np.ascontiguousarray(np.asarray(L0_ffn_W1, dtype=np.float32))
    W2 = np.ascontiguousarray(np.asarray(L0_ffn_W2, dtype=np.float32))
    W3 = np.ascontiguousarray(np.asarray(L0_ffn_W3, dtype=np.float32))

    groups, bvp, wcat, w2t = _host_prepare(bv, cols, rows, W1, W2, W3)
    nc = _build(groups, rows)

    hc_np = np.ascontiguousarray(
        np.kron(_hada(64), _hada(2)).astype(ml_dtypes.bfloat16))
    id_np = np.eye(128, dtype=ml_dtypes.bfloat16)

    in_maps = []
    for core in range(8):
        bi, hf = core // 2, core % 2
        xg = (1024 * hf + np.arange(8)[None, :] * 128
              + np.arange(128)[:, None]).astype(np.int32)
        in_maps.append({
            "xb": x[bi],
            "bvp": bvp,
            "wcat": wcat,
            "w2t": w2t,
            "hcin": hc_np,
            "idin": id_np,
            "xgidx": np.ascontiguousarray(xg),
        })
    res = run_bass_kernel_spmd(nc, in_maps, core_ids=list(range(8)))
    outp = np.empty((4, 2048, 2048), np.float32)
    for core in range(8):
        bi, hf = core // 2, core % 2
        outp[bi, hf * 1024:(hf + 1) * 1024, :] = res.results[core]["out"]
    return outp
